# version 41
# baseline (speedup 1.0000x reference)
import numpy as np
import ml_dtypes
import concourse.bass as bass
import concourse.mybir as mybir
from concourse.bass_utils import run_bass_kernel_spmd
from concourse.aot_env import allow_ici_all_to_all
from contextlib import ExitStack

F32 = mybir.dt.float32
BF16 = mybir.dt.bfloat16
I32 = mybir.dt.int32
U8 = mybir.dt.uint8
AF = mybir.ActivationFunctionType
ALU = mybir.AluOpType
X = mybir.AxisListType.X

D = 1024
HID = 3072
NCORE = 8
SCALE = 0.125
DEQ = 1.0 / 127.0
ENGS = ("pe", "act", "dve", "hw", "sw")


class Arena:
    def __init__(self, size):
        self.size = size
        self.blocks = {}
        self.free_list = [(0, size)]

    def alloc(self, name, nbytes, align=32):
        assert name not in self.blocks
        nbytes = (nbytes + align - 1) // align * align
        for i, (off, sz) in enumerate(self.free_list):
            if nbytes <= sz:
                repl = [(off + nbytes, sz - nbytes)] if sz > nbytes else []
                self.free_list[i:i + 1] = repl
                self.blocks[name] = (off, nbytes)
                return off
        raise MemoryError(f"arena: no room for {name} ({nbytes}B); free={self.free_list}")

    def free(self, name):
        off, sz = self.blocks.pop(name)
        self.free_list.append((off, sz))
        self.free_list.sort()
        out = []
        for b in self.free_list:
            if out and out[-1][0] + out[-1][1] == b[0]:
                out[-1] = (out[-1][0], out[-1][1] + b[1])
            else:
                out.append(b)
        self.free_list = out

    def used(self):
        return self.size - sum(s for _, s in self.free_list)


class Plan:

    def __init__(self):
        self.q = {e: [] for e in ENGS}
        self.n = dict(pe=0, act=0, dve=0, hw=0, sw=0, cc=0, hwc=0, swc=0)

    def wait(self, eng, sname, val):
        if val <= 0:
            return
        if sname in ("hw", "sw"):
            val *= 16
        self.q[eng].append(("w", sname, val))

    def finalize_device(self):
        waited = {}
        for e in ENGS:
            for it in self.q[e]:
                if it[0] == "w":
                    waited.setdefault(it[1], set()).add(it[2])
        remap = {}
        for e in ("dve", "act"):
            w = waited.get(e, set())
            old = new = 0
            m = {0: 0}
            outq = []
            for it in self.q[e]:
                if it[0] == "o" and it[4] == e:
                    old += 1
                    if old in w:
                        new += 1
                        outq.append(it)
                    else:
                        outq.append((it[0], it[1], it[2], it[3], None))
                    m[old] = new
                else:
                    outq.append(it)
            self.q[e] = outq
            remap[e] = m
        for e in ENGS:
            outq = []
            for it in self.q[e]:
                if it[0] == "w" and it[1] in remap:
                    outq.append(("w", it[1], remap[it[1]][it[2]]))
                else:
                    outq.append(it)
            self.q[e] = outq

    def edge(self, eng):
        self.wait(eng, eng, self.n[eng])

    def ckpt(self, q):
        self.wait(q, q, self.n[q])
        self.n[q + "c"] += 1
        self.q[q].append(("o", "nop", (), {}, q + "c"))
        return self.n[q + "c"]

    def op(self, eng, method, *args, inc=None, **kw):
        if eng in ("dve", "act"):
            inc = eng
        if inc is not None:
            self.n[inc] += 1
        self.q[eng].append(("o", method, args, kw, inc))


def build_nc(T, dense_sync=False):
    assert T % 1024 == 0
    TOKG = 2 * T
    TOKC = TOKG // NCORE
    NT = T // 512
    NSC = T // 1024
    NKB = T // 128
    SHARD = TOKC
    assert SHARD % 512 == 0
    NTC = TOKC // 512
    LNW = min(1024, TOKC)
    NPASS = TOKC // LNW
    NU = LNW // 512
    NCH = 2 * NSC

    nc = bass.Bass("TRN2", target_bir_lowering=False)

    i_xT = [nc.dram_tensor(f"xT{b}", [128, 8 * T], F32,
                           kind="ExternalInput").ap() for b in range(2)]
    i_xres = nc.dram_tensor("xres", [D, TOKC], F32, kind="ExternalInput").ap()
    i_wq = nc.dram_tensor("wq", [128, D], F32, kind="ExternalInput").ap()
    i_wk = nc.dram_tensor("wk", [128, D], F32, kind="ExternalInput").ap()
    i_wv = nc.dram_tensor("wv", [128, D], F32, kind="ExternalInput").ap()
    i_bq = nc.dram_tensor("bq", [128, 1], F32, kind="ExternalInput").ap()
    i_bk = nc.dram_tensor("bk", [128, 1], F32, kind="ExternalInput").ap()
    i_bv = nc.dram_tensor("bvrow", [1, 128], F32, kind="ExternalInput").ap()
    i_wo = nc.dram_tensor("wo", [D, D], F32, kind="ExternalInput").ap()
    i_bo = nc.dram_tensor("bo", [128, 8], F32, kind="ExternalInput").ap()
    i_wgT = nc.dram_tensor("wgT", [D, HID], I32, kind="ExternalInput").ap()
    i_wuT = nc.dram_tensor("wuT", [D, HID], I32, kind="ExternalInput").ap()
    i_wdT = nc.dram_tensor("wdT", [HID, D], I32, kind="ExternalInput").ap()
    i_masks = nc.dram_tensor("masks", [512, 1024], BF16, kind="ExternalInput").ap()
    i_ones = nc.dram_tensor("onescol", [128, 1], F32, kind="ExternalInput").ap()
    i_onesr = nc.dram_tensor("onesrow", [1, 128], F32, kind="ExternalInput").ap()
    i_sel2 = nc.dram_tensor("sel2", [2, 128], F32, kind="ExternalInput").ap()
    o_out = nc.dram_tensor("outT", [D, TOKC], F32, kind="ExternalOutput").ap()
    NCHUNK = 2 if SHARD >= 1024 else 1
    HSH = SHARD // NCHUNK
    d_ain = [nc.dram_tensor(f"a2a_in{x}", [8 * 130, HSH], BF16, kind="Internal")
             for x in range(NCHUNK)]
    d_aout = [nc.dram_tensor(f"a2a_out{x}", [8 * 130, HSH], BF16, kind="Internal")
              for x in range(NCHUNK)]

    AB = 207 * 1024
    ar = Arena(AB)
    P = Plan()

    ctx = ExitStack()
    with ctx:
        e = ctx.enter_context
        e(nc.allow_low_precision(
            reason="bf16 matmul operands with f32 accumulation; explicitly "
                   "sanctioned by this problem's stated tolerance"))
        arena = e(nc.sbuf_tensor("arena", [128, AB], U8))
        psA = e(nc.psum_tensor("psA", [128, 2048], F32))
        psB = e(nc.psum_tensor("psB", [128, 2048], F32))
        sem = {k: e(nc.semaphore(f"s_{k}"))
               for k in ("pe", "act", "dve", "hw", "sw", "cc", "hwc", "swc")}

        def V(name, dtype, cols, parts=128, p0=0):
            off, nb = ar.blocks[name]
            esz = mybir.dt.size(dtype)
            assert cols * esz <= nb, (name, cols * esz, nb)
            return arena[p0:p0 + parts, off:off + cols * esz].bitcast(dtype)

        for nm, nb in (("bqkv", 12), ("onescol", 2), ("biasc", 8),
                       ("bvb", 512), ("bvb4", 2048), ("bo", 32), ("sel2", 256), ("onesrow", 256),
                       ("bvrow", 256), ("eps", 4), ("ntmp", 8 * 16 * 4),
                       ("scr", 64), ("wqb", 2048), ("wkb", 2048), ("wvb", 2048)):
            ar.alloc(nm, nb)
        TF = max(T, 8192)
        for b in range(2):
            ar.alloc(f"qt{b}", TF * 2)
            ar.alloc(f"kt{b}", TF * 2)
        NUNIT = 2 * NKB * 2
        ar.alloc("vsb", 2 * (TF // 128) * 2 * 65 * 2)
        ar.alloc("xt0", 16384)
        ar.alloc("xt1", 16384)
        ar.alloc("xt2", 16384)
        ar.alloc("pbuf", 8192)
        ar.alloc("ptmp", 4096)
        ar.alloc("sq0", 1024)
        ar.alloc("sq1", 1024)
        ar.alloc("maskt", 8192)
        ar.alloc("hobuf", 2048)
        ar.alloc("recf", 4096)
        ar.alloc("recb", 4096)
        ar.alloc("wo", 16384)
        usedA = ar.used()

        v_ones = V("onescol", BF16, 1)
        v_onesr = V("onesrow", BF16, 128, parts=1)
        v_sel2 = V("sel2", BF16, 128, parts=2)
        v_bvrow = V("bvrow", BF16, 128, parts=1)
        v_bvb = V("bvb", F32, 128)
        v_bvb4 = V("bvb4", F32, 512).rearrange("p (r c) -> p r c", r=4)
        v_bqkv = V("bqkv", F32, 3)
        v_biasc = V("biasc", F32, 2)
        v_bo = V("bo", F32, 8)
        v_eps = V("eps", F32, 1, parts=1)
        v_ntmp = V("ntmp", F32, 8 * NCH, parts=1)
        v_scr = V("scr", F32, 16, parts=1)
        v_wq = V("wqb", BF16, 1024).rearrange("p (d c) -> p d c", d=8)
        v_wk = V("wkb", BF16, 1024).rearrange("p (d c) -> p d c", d=8)
        v_wv = V("wvb", BF16, 1024).rearrange("p (d c) -> p d c", d=8)
        v_qt = [V(f"qt{b}", BF16, T) for b in range(2)]
        v_kt = [V(f"kt{b}", BF16, T) for b in range(2)]
        v_vsb = V("vsb", BF16, NUNIT * 65).rearrange("p (u c) -> p u c", u=NUNIT)
        v_xt = [V("xt0", BF16, 8192).rearrange("p (d t) -> p d t", d=8),
                V("xt1", BF16, 8192).rearrange("p (d t) -> p d t", d=8),
                V("xt2", BF16, 8192).rearrange("p (d t) -> p d t", d=8)]
        v_xt_flat = [V("xt0", BF16, 8192), V("xt1", BF16, 8192),
                     V("xt2", BF16, 8192)]
        v_pbuf = V("pbuf", BF16, 4096)
        v_ptmp = V("ptmp", BF16, 2048)
        v_sq = [V("sq0", BF16, 512), V("sq1", BF16, 512)]
        v_maskt = V("maskt", BF16, 4096).rearrange("p (j q) -> p j q", j=4)
        v_hob = V("hobuf", BF16, 1024)
        v_recf = V("recf", F32, 1024, parts=1)
        v_recb = V("recb", BF16, 2048, parts=1)
        v_wo = V("wo", BF16, 8192).rearrange("p (i c) -> p i c", i=8)

        P.op("hw", "dma_start", inc="hw",
             out=V("maskt", BF16, 4096).rearrange("p (j q) -> p j q", j=4),
             in_=i_masks.rearrange("(j p) q -> p j q", p=128))
        P.op("hw", "dma_start", inc="hw", out=v_bqkv[:, 0:1], in_=i_bq)
        P.op("hw", "dma_start", inc="hw", out=v_bqkv[:, 1:2], in_=i_bk)
        P.op("hw", "dma_start", inc="hw", out=v_bo, in_=i_bo)
        hw_consts = P.ckpt("hw")
        sc_list = [(b, sc) for b in range(2) for sc in range(NSC)]
        xt_ready = {}

        def emit_xt_load(idx, wait_pe):
            b, sc = sc_list[idx]
            P.wait("sw", "pe", wait_pe)
            P.op("sw", "dma_start", inc="sw", out=v_xt_flat[idx % 3],
                 in_=i_xT[b][:, 8192 * sc:8192 * sc + 8192])
            ck = P.ckpt("sw")
            xt_ready[idx] = [ck, ck]

        P.op("sw", "dma_start", inc="sw", out=v_xt_flat[0],
             in_=i_xT[0][:, 0:8192])
        P.op("sw", "dma_start", inc="sw", out=V("wqb", BF16, 1024), in_=i_wq)
        P.op("sw", "dma_start", inc="sw", out=V("wkb", BF16, 1024), in_=i_wk)
        P.op("sw", "dma_start", inc="sw", out=V("wvb", BF16, 1024), in_=i_wv)
        ck0 = P.ckpt("sw")
        xt_ready[0] = [ck0, ck0]
        P.op("sw", "dma_start", inc="sw", out=v_ones, in_=i_ones)
        P.op("sw", "dma_start", inc="sw", out=v_onesr, in_=i_onesr)
        P.op("sw", "dma_start", inc="sw", out=v_sel2, in_=i_sel2)
        P.op("sw", "dma_start", inc="sw", out=v_bvrow, in_=i_bv)
        sw_consts = P.ckpt("sw")

        P.wait("dve", "hwc", hw_consts)
        P.wait("act", "hwc", hw_consts)
        P.op("dve", "memset", v_eps[:], 1e-5)
        P.op("dve", "memset", v_ptmp[:], 0.0)
        P.op("dve", "memset", v_vsb[:, :, 64:65], 1.0)
        P.wait("pe", "swc", sw_consts)
        P.op("pe", "matmul", psB[:, 0:128], v_onesr, v_bvrow,
             start=True, stop=True, inc="pe")
        P.wait("dve", "pe", P.n["pe"])
        P.op("dve", "tensor_copy", v_bvb, psB[:, 0:128])
        P.edge("dve")
        for r_ in range(4):
            P.op("dve", "tensor_copy", v_bvb4[:, r_, :], v_bvb)
        dve_bvb = P.n["dve"]

        if len(sc_list) > 1:
            emit_xt_load(1, 0)
        if len(sc_list) > 2:
            emit_xt_load(2, 0)

        wo_loaded = [False]
        prev_qk_act = 0
        prev_v_ev = {0: dve_bvb, 1: dve_bvb}
        prev_norm_ev = 0
        for idx, (b, sc) in enumerate(sc_list):
            slot = idx % 3
            pe_v_last = 0
            for tc2 in range(2):
                P.wait("pe", "swc", xt_ready[idx][tc2])
                tt = 1024 * sc + 512 * tc2
                tl = 512 * tc2
                P.wait("pe", "act", prev_qk_act)
                for d in range(8):
                    P.op("pe", "matmul", psA[:, 0:512], v_wq[:, d, :],
                         v_xt[slot][:, d, tl:tl + 512],
                         start=(d == 0), stop=(d == 7),
                         inc="pe" if d == 7 else None)
                pe_q = P.n["pe"]
                for d in range(8):
                    P.op("pe", "matmul", psA[:, 512:1024], v_wk[:, d, :],
                         v_xt[slot][:, d, tl:tl + 512],
                         start=(d == 0), stop=(d == 7),
                         inc="pe" if d == 7 else None)
                pe_k = P.n["pe"]
                P.wait("act", "pe", pe_q)
                P.op("act", "activation", v_sq[0][:], psA[:, 0:512], AF.Square,
                     bias=v_bqkv[:, 0:1])
                P.op("act", "add", v_qt[b][:, tt:tt + 512],
                     psA[:, 0:512], v_bqkv[:, 0:1])
                act_q = P.n["act"]
                P.wait("act", "pe", pe_k)
                P.op("act", "activation", v_sq[1][:], psA[:, 512:1024], AF.Square,
                     bias=v_bqkv[:, 1:2])
                P.op("act", "add", v_kt[b][:, tt:tt + 512],
                     psA[:, 512:1024], v_bqkv[:, 1:2])
                act_k = P.n["act"]
                prev_qk_act = act_k
                vb = tc2
                P.wait("pe", "dve", prev_v_ev[vb])
                for tb in range(4):
                    for d in range(8):
                        P.op("pe", "matmul",
                             psB[:, 512 * vb + 128 * tb:512 * vb + 128 * tb + 128],
                             v_xt[slot][:, d, tl + 128 * tb:tl + 128 * tb + 128],
                             v_wv[:, d, :],
                             start=(tb == 0 and d == 0),
                             stop=(tb == 3 and d == 7), skip_group_check=True,
                             inc="pe" if (tb == 3 and d == 7) else None)
                pe_v = P.n["pe"]
                pe_v_last = pe_v
                P.wait("pe", "act", act_k)
                P.wait("pe", "dve", prev_norm_ev)
                ndst = [psA[0:1, 1024:1536], psA[0:1, 1536:2048],
                        psB[0:1, 1024:1536], psB[0:1, 1536:2048]]
                for v in range(4):
                    half, sq = v % 2, v // 2
                    P.op("pe", "matmul", ndst[v],
                         v_ones[64 * half:64 * half + 64, :],
                         v_sq[sq][64 * half:64 * half + 64, :],
                         start=True, stop=True,
                         tile_position=(64 * half, 0),
                         inc="pe" if v == 3 else None)
                pe_norm = P.n["pe"]
                P.wait("dve", "pe", pe_norm)
                ch = sc * 2 + tc2
                for v in range(4):
                    dst = (b * 4 + v) * NCH + ch
                    P.op("dve", "reduce_max", v_ntmp[:, dst:dst + 1],
                         ndst[v], axis=X)
                prev_norm_ev = P.n["dve"]
                P.wait("dve", "pe", pe_v)
                kb0 = tt // 128
                u0 = (b * NKB + kb0) * 2
                for hh in range(2):
                    P.op("dve", "tensor_tensor",
                         v_vsb[:, u0 + hh:u0 + hh + 7:2, 0:64],
                         psB[:, 512 * vb:512 * vb + 512].rearrange(
                             "p (tb c) -> p tb c", tb=4)[:, :, 64 * hh:
                                                         64 * hh + 64],
                         v_bvb4[:, :, 64 * hh:64 * hh + 64], ALU.add)
                prev_v_ev[vb] = P.n["dve"]
            if idx + 3 < len(sc_list):
                emit_xt_load(idx + 3, pe_v_last)

        P.edge("dve")
        for b in range(2):
            for v in range(4):
                base = (b * 4 + v) * NCH
                P.op("dve", "tensor_reduce", v_scr[:, v:v + 1],
                     v_ntmp[:, base:base + NCH], X, ALU.max)
            P.edge("dve")
            P.op("dve", "tensor_tensor", v_scr[:, 4:5], v_scr[:, 0:1],
                 v_scr[:, 2:3], ALU.mult)
            P.op("dve", "tensor_tensor", v_scr[:, 5:6], v_scr[:, 1:2],
                 v_scr[:, 3:4], ALU.mult)
            P.edge("dve")
            P.op("dve", "tensor_tensor", v_scr[:, 6:7], v_scr[:, 4:5],
                 v_scr[:, 5:6], ALU.max)
            P.wait("act", "dve", P.n["dve"])
            P.op("act", "activation", v_scr[:, 7:8], v_scr[:, 6:7], AF.Sqrt)
            P.wait("dve", "act", P.n["act"])
            P.op("dve", "tensor_scalar_mul", v_scr[:, 7:8], v_scr[:, 7:8],
                 -(SCALE * 1.0103))
            P.edge("dve")
            vb16 = v_scr[:, 12:13].bitcast(BF16)[:, 0:1]
            P.op("dve", "tensor_copy", vb16, v_scr[:, 7:8])
            P.wait("pe", "dve", P.n["dve"])
            P.op("pe", "matmul", psB[:, 128:129], v_onesr, vb16,
                 start=True, stop=True, inc="pe")
            P.wait("dve", "pe", P.n["pe"])
            P.op("dve", "tensor_copy", v_biasc[:, b:b + 1], psB[:, 128:129])
        dve_bias = P.n["dve"]

        xt0_off = ar.blocks["xt0"][0]
        ar.free("xt0")
        ar.free("xt1")
        ar.free("xt2")
        gsb_off = ar.alloc("gsb", 8 * 4096)
        assert gsb_off == xt0_off, (gsb_off, xt0_off)
        ar.alloc("grecip", 8 * LNW * 2)
        v_gsb_early = V("gsb", BF16, 8 * 2048).rearrange("p (i t) -> p i t", i=8)
        v_grec_early = V("grecip", BF16, 8 * LNW,
                         parts=2).rearrange("p (i t) -> p i t", i=8)
        early_gsb = {}
        early_grec = {}
        P.wait("act", "dve", dve_bias)
        P.wait("pe", "dve", dve_bias)
        g = 0
        act_of_g = {}
        mask_of_g = {}
        ptmp_user = {}
        tile_ep = {}
        hw_tile = {}
        tidx = 0
        ain = [x.ap() for x in d_ain]

        def chunk_of(b, t):
            return ((b * T + 512 * t) % SHARD) // HSH
        tile_list = [(b, t) for c in range(NCHUNK)
                     for b in range(2) for t in range(NT) if chunk_of(b, t) == c]
        n_tiles_A = (sum(1 for b, t in tile_list if chunk_of(b, t) == 0)
                     if NCHUNK == 2 else -1)
        pending_tail = [None]
        for ti, (b, t) in enumerate(tile_list):
            if True:
                nkb = 4 * t + 4
                bko = 1024 * (ti % 2)
                gbase = g

                def emit_av(pkb, gbase=gbase, nkb=nkb, bko=bko, b=b, ti=ti):
                    psl = (gbase + pkb) % 4
                    if pkb == 0:
                        P.wait("pe", "dve", tile_ep.get(ti - 2, 0))
                    gg = gbase + pkb
                    if gg in mask_of_g:
                        P.wait("pe", "dve", mask_of_g[gg])
                    else:
                        P.wait("pe", "act", act_of_g[gg])
                    for hh in range(2):
                        u = (b * NKB + pkb) * 2 + hh
                        P.op("pe", "matmul",
                             psB[0:65, bko + 512 * hh:bko + 512 * hh + 512],
                             v_vsb[:, u, :],
                             v_pbuf[:, 1024 * psl + 512 * hh:
                                    1024 * psl + 512 * hh + 512],
                             start=(pkb == 0), stop=(pkb == nkb - 1),
                             skip_group_check=True,
                             inc="pe" if (hh == 1 and pkb == nkb - 1) else None)

                for kb in range(nkb):
                    sl2 = g % 2
                    sl3 = g % 4
                    diag = kb >= 4 * t
                    j = kb - 4 * t
                    if g >= 2:
                        P.wait("pe", "act", act_of_g[g - 2])
                    P.op("pe", "matmul", psA[:, 1024 * sl2:1024 * sl2 + 512],
                         v_kt[b][0:64, 128 * kb:128 * kb + 128],
                         v_qt[b][0:64, 512 * t:512 * t + 512],
                         start=True, stop=True, tile_position=(0, 0))
                    P.op("pe", "matmul", psA[:, 1024 * sl2 + 512:1024 * sl2 + 1024],
                         v_kt[b][64:128, 128 * kb:128 * kb + 128],
                         v_qt[b][64:128, 512 * t:512 * t + 512],
                         start=True, stop=True, tile_position=(64, 0), inc="pe")
                    pe_s = P.n["pe"]
                    P.wait("act", "pe", pe_s)
                    if diag:
                        pts = g % 2
                        P.wait("act", "dve", ptmp_user.get(pts, 0))
                        if j >= 2:
                            c = 128 * j
                            for hh in range(2):
                                P.op("act", "activation",
                                     v_ptmp[:, 1024 * pts + 512 * hh + c:
                                            1024 * pts + 512 * hh + 512],
                                     psA[:, 1024 * sl2 + 512 * hh + c:
                                         1024 * sl2 + 512 * hh + 512],
                                     AF.Exp, bias=v_biasc[:, b:b + 1],
                                     scale=SCALE)
                        else:
                            P.op("act", "activation",
                                 v_ptmp[:, 1024 * pts:1024 * pts + 1024],
                                 psA[:, 1024 * sl2:1024 * sl2 + 1024], AF.Exp,
                                 bias=v_biasc[:, b:b + 1], scale=SCALE)
                    else:
                        P.op("act", "activation",
                             v_pbuf[:, 1024 * sl3:1024 * sl3 + 1024],
                             psA[:, 1024 * sl2:1024 * sl2 + 1024], AF.Exp,
                             bias=v_biasc[:, b:b + 1], scale=SCALE)
                    act_of_g[g] = P.n["act"]
                    if diag:
                        P.wait("dve", "act", act_of_g[g])
                        P.op("dve", "tensor_tensor",
                             v_pbuf[:, 1024 * sl3:1024 * sl3 + 1024],
                             v_ptmp[:, 1024 * pts:1024 * pts + 1024],
                             v_maskt[:, j, :], ALU.mult)
                        mask_of_g[g] = P.n["dve"]
                        ptmp_user[pts] = P.n["dve"]
                    if kb >= 2:
                        emit_av(kb - 2)
                    g += 1
                    if kb == 1 and pending_tail[0] is not None:
                        pending_tail[0]()
                        pending_tail[0] = None

                def tile_tail(emit_av=emit_av, nkb=nkb, bko=bko, b=b, t=t,
                              ti=ti):
                    emit_av(nkb - 2)
                    emit_av(nkb - 1)
                    pe_av = P.n["pe"]
                    hsl = ti % 2
                    last_tile = (ti == len(tile_list) - 1)
                    P.wait("dve", "pe", pe_av)
                    if ti >= 2:
                        P.wait("dve", "hwc", hw_tile[ti - 2])
                    P.edge("dve")
                    if last_tile:
                        P.wait("act", "pe", pe_av)
                        if ti >= 1:
                            P.wait("act", "dve", tile_ep[ti - 1])
                        if ti >= 2:
                            P.wait("act", "hwc", hw_tile[ti - 2])
                        P.op("act", "activation", v_recf[:, 0:1024],
                             psB[64:65, bko:bko + 1024], AF.Ln)
                        P.edge("act")
                        P.op("act", "activation",
                             v_recb[:, 1024 * hsl:1024 * hsl + 1024],
                             v_recf[:, 0:1024], AF.Exp, scale=-1.0)
                    else:
                        P.op("dve", "reciprocal", v_recf[:, 0:512],
                             psB[64:65, bko:bko + 512])
                        P.edge("dve")
                        P.op("dve", "tensor_copy",
                             v_recb[:, 1024 * hsl:1024 * hsl + 512],
                             v_recf[:, 0:512])
                        P.op("dve", "reciprocal", v_recf[:, 512:1024],
                             psB[64:65, bko + 512:bko + 1024])
                        P.edge("dve")
                        P.op("dve", "tensor_copy",
                             v_recb[:, 1024 * hsl + 512:1024 * hsl + 1024],
                             v_recf[:, 512:1024])
                    P.op("dve", "tensor_copy",
                         v_hob[0:64, 512 * hsl:512 * hsl + 512],
                         psB[0:64, bko:bko + 512])
                    P.op("dve", "tensor_copy",
                         v_hob[64:128, 512 * hsl:512 * hsl + 512],
                         psB[0:64, bko + 512:bko + 1024])
                    tile_ep[ti] = P.n["dve"]
                    gt0 = b * T + 512 * t
                    sh = gt0 // SHARD
                    ch = chunk_of(b, t)
                    c0 = (gt0 % SHARD) % HSH
                    P.wait("hw", "dve", tile_ep[ti])
                    if last_tile:
                        P.wait("hw", "act", P.n["act"])
                    P.op("hw", "dma_start", inc="hw",
                         out=ain[ch][130 * sh:130 * sh + 128, c0:c0 + 512],
                         in_=v_hob[:, 512 * hsl:512 * hsl + 512])
                    P.op("hw", "dma_start", inc="hw",
                         out=ain[ch][130 * sh + 128:130 * sh + 129, c0:c0 + 512],
                         in_=v_recb[:, 1024 * hsl:1024 * hsl + 512])
                    P.op("hw", "dma_start", inc="hw",
                         out=ain[ch][130 * sh + 129:130 * sh + 130, c0:c0 + 512],
                         in_=v_recb[:, 1024 * hsl + 512:1024 * hsl + 1024])
                    hw_tile[ti] = P.ckpt("hw")
                    if ti + 1 == n_tiles_A:
                        P.wait("sw", "hwc", hw_tile[ti])

                        def _ccA(h):
                            with allow_ici_all_to_all():
                                return h.collective_compute(
                                    "AllToAll", ALU.bypass,
                                    replica_groups=[list(range(NCORE))],
                                    ins=[d_ain[0].ap()], outs=[d_aout[0].ap()])
                        P.q["sw"].append(("raw", _ccA, "cc"))
                        P.n["cc"] += 1
                        P.op("sw", "dma_start", inc="sw", out=v_wo,
                             in_=i_wo.rearrange("(i p) c -> p i c", p=128))
                        wo_loaded[0] = True
                        if NCHUNK == 2 and NPASS == 2:
                            P.wait("sw", "cc", 1)
                            for i_ in range(8):
                                P.op("sw", "dma_start", inc="sw",
                                     out=v_gsb_early[:, i_, 0:HSH],
                                     in_=d_aout[0].ap()
                                     [130 * i_:130 * i_ + 128, :])
                            early_gsb[0] = (0, P.ckpt("sw"))
                            P.op("sw", "dma_start", inc="sw",
                                 out=v_grec_early[:, :, 0:HSH],
                                 in_=d_aout[0].ap().rearrange(
                                     "(i q) t -> q i t", q=130)[128:130, :, :])
                            early_grec[0] = P.ckpt("sw")
                pending_tail[0] = tile_tail
        pending_tail[0]()
        pending_tail[0] = None
        dve_tilefree = tile_ep[len(tile_list) - 1]
        hw_payload = P.n["hwc"]
        act_attn_done = P.n["act"]
        dve_attn_done = dve_tilefree

        P.wait("sw", "hwc", hw_payload)

        def _ccB(h):
            with allow_ici_all_to_all():
                return h.collective_compute(
                    "AllToAll", ALU.bypass,
                    replica_groups=[list(range(NCORE))],
                    ins=[d_ain[-1].ap()], outs=[d_aout[-1].ap()])
        P.q["sw"].append(("raw", _ccB, "cc"))
        P.n["cc"] += 1

        pe_attn_done = P.n["pe"]
        for nm in ("qt0", "kt0", "qt1", "kt1", "vsb", "pbuf",
                   "ptmp", "sq0", "sq1", "hobuf", "recf", "recb", "maskt",
                   "wqb", "wkb", "wvb"):
            ar.free(nm)
        FW = LNW
        NSP = 1
        ar.alloc("rr", 8 * TOKC * 4)
        ar.alloc("tmp", LNW * 4)
        ar.alloc("xres", 4 * LNW * 4)
        ar.alloc("rbf", 2 * LNW * 2)
        ar.alloc("sqb", 2 * LNW * 2)
        ar.alloc("rowa", LNW * 4)
        ar.alloc("rowb", LNW * 4)
        ar.alloc("rowc", LNW * 4)
        ar.alloc("rowbf", 2 * LNW * 2)
        ar.alloc("mbc", 2 * LNW * 2)
        ar.alloc("rbc", 2 * LNW * 2)
        usedB = ar.used()

        v_gsb = V("gsb", BF16, 8 * 2048).rearrange("p (i t) -> p i t", i=8)
        v_rr = V("rr", F32, 8 * TOKC).rearrange("p (i t) -> p i t", i=8)
        v_tmp = V("tmp", F32, LNW)
        v_xres = V("xres", F32, 4 * LNW).rearrange("p (s t) -> p s t", s=4)
        v_grec = V("grecip", BF16, 8 * LNW, parts=2).rearrange("p (i t) -> p i t", i=8)
        v_rbf = V("rbf", BF16, 2 * LNW).rearrange("p (s t) -> p s t", s=2)
        v_sqb = V("sqb", BF16, 2 * LNW).rearrange("p (s t) -> p s t", s=2)
        v_rowa = V("rowa", F32, LNW, parts=1)
        v_rowb = V("rowb", F32, LNW, parts=1)
        v_rowc = V("rowc", F32, LNW, parts=1)
        v_rowbf = V("rowbf", BF16, 2 * LNW, parts=1)
        v_mbc = V("mbc", BF16, 2 * LNW)
        v_rbc = V("rbc", BF16, 2 * LNW)

        aout = [x.ap() for x in d_aout]
        xres_r = i_xres.rearrange("(j p) t -> p j t", p=128)
        outr = o_out.rearrange("(j p) t -> p j t", p=128)

        ln_slot_ctr = [0]

        def emit_ln_stats(src_of_j, off, width, pre_dve, prev_stats_pe):
            nu = width // 512
            slot = ln_slot_ctr[0] % 2
            ln_slot_ctr[0] += 1
            stats_mm = {}
            for j in range(8):
                gate_pe = stats_mm[j - 2] if j >= 2 else prev_stats_pe
                P.wait("dve", "pe", gate_pe)
                P.wait("dve", "dve", pre_dve)
                P.op("dve", "tensor_copy", v_rbf[:, j % 2, 0:width],
                     src_of_j(j)[:, off:off + width])
                rbf_d = P.n["dve"]
                P.wait("act", "pe", gate_pe)
                P.wait("act", "dve", pre_dve)
                P.op("act", "activation", v_sqb[:, j % 2, 0:width],
                     src_of_j(j)[:, off:off + width], AF.Square)
                sq_d = P.n["act"]
                P.wait("pe", "dve", rbf_d)
                P.wait("pe", "act", sq_d)
                for u in range(nu):
                    P.op("pe", "matmul", psB[0:1, 512 * u:512 * u + 512],
                         v_ones, v_rbf[:, j % 2, 512 * u:512 * u + 512],
                         start=(j == 0), stop=(j == 7), skip_group_check=True)
                    P.op("pe", "matmul", psA[0:1, 512 * u:512 * u + 512],
                         v_ones, v_sqb[:, j % 2, 512 * u:512 * u + 512],
                         start=(j == 0), stop=(j == 7), skip_group_check=True,
                         inc="pe" if u == nu - 1 else None)
                stats_mm[j] = P.n["pe"]
            pe_stats = P.n["pe"]
            P.wait("dve", "pe", pe_stats)
            P.op("dve", "tensor_scalar_mul", v_rowa[:, 0:width],
                 psB[0:1, 0:width], 1.0 / D)
            P.edge("dve")
            P.op("dve", "tensor_tensor", v_rowb[:, 0:width], v_rowa[:, 0:width],
                 v_rowa[:, 0:width], ALU.mult)
            P.edge("dve")
            P.op("dve", "scalar_tensor_tensor", v_rowc[:, 0:width],
                 psA[0:1, 0:width], 1.0 / D, v_rowb[:, 0:width],
                 ALU.mult, ALU.subtract)
            dvar = P.n["dve"]
            P.wait("act", "dve", dvar)
            P.op("act", "activation", v_rowb[:, 0:width], v_rowc[:, 0:width],
                 AF.Ln, bias=v_eps[:])
            P.edge("act")
            mrow = v_rowbf[:, 0:width]
            rrow = v_rowbf[:, LNW:LNW + width]
            P.op("act", "activation", rrow, v_rowb[:, 0:width],
                 AF.Exp, scale=-0.5)
            P.op("act", "activation", mrow, v_rowa[:, 0:width], AF.Copy)
            a_rows = P.n["act"]
            P.wait("pe", "act", a_rows)
            for u in range(width // 512):
                P.op("pe", "matmul", psA[:, 512 * u:512 * u + 512], v_onesr,
                     mrow[:, 512 * u:512 * u + 512], start=True, stop=True)
                P.op("pe", "matmul", psB[:, 512 * u:512 * u + 512], v_onesr,
                     rrow[:, 512 * u:512 * u + 512], start=True, stop=True,
                     inc="pe" if u == width // 512 - 1 else None)
            pe_bcast = P.n["pe"]
            P.wait("dve", "pe", pe_bcast)
            P.op("dve", "tensor_copy", v_mbc[:, LNW * slot:LNW * slot + width],
                 psA[:, 0:width])
            P.op("dve", "tensor_copy", v_rbc[:, LNW * slot:LNW * slot + width],
                 psB[:, 0:width])
            return P.n["dve"], slot

        def norm_chunk(src_of_j, off, width, slot, j, dst_bf_of_j=None,
                       out_dma_of_j=None):
            P.edge("dve")
            P.op("dve", "tensor_tensor", v_tmp[0:128, 0:width],
                 src_of_j(j)[:, off:off + width],
                 v_mbc[:, LNW * slot:LNW * slot + width], ALU.subtract)
            P.edge("dve")
            P.op("dve", "tensor_tensor", src_of_j(j)[:, off:off + width],
                 v_tmp[0:128, 0:width],
                 v_rbc[:, LNW * slot:LNW * slot + width], ALU.mult)
            nrm_j = P.n["dve"]
            if dst_bf_of_j is not None:
                P.wait("act", "dve", nrm_j)
                P.op("act", "activation",
                     dst_bf_of_j(j)[:, off:off + width],
                     src_of_j(j)[:, off:off + width], AF.Copy)
            if out_dma_of_j is not None:
                P.wait("hw", "dve", nrm_j)
                P.op("hw", "dma_start", inc="hw",
                     out=out_dma_of_j(j), in_=src_of_j(j)[:, off:off + width])

        def emit_ln_norm(src_of_j, off, width, slot, dst_bf_of_j=None,
                         out_dma_of_j=None):
            for j in range(8):
                norm_chunk(src_of_j, off, width, slot, j, dst_bf_of_j,
                           out_dma_of_j)
            if dst_bf_of_j is not None:
                P.wait("dve", "act", P.n["act"])
                P.op("dve", "engine_nop")
            return P.n["dve"]

        def emit_ln(src_of_j, off, width, pre_dve, prev_stats_pe,
                    dst_bf_of_j=None, out_dma_of_j=None):
            bc_done, slot = emit_ln_stats(src_of_j, off, width, pre_dve,
                                          prev_stats_pe)
            done = emit_ln_norm(src_of_j, off, width, slot, dst_bf_of_j,
                                out_dma_of_j)
            return done, bc_done

        ar.alloc("silb", 2 * FW * 2)
        ar.alloc("ffnh0", FW * 2)
        ar.alloc("ffnh1", FW * 2)
        ar.alloc("wgu0", 8192)
        ar.free("grecip")
        ar.alloc("wgu1", 8192)
        ar.free("wo")
        ar.free("xres")
        for j_ in range(2, 24):
            ar.alloc(f"ffnh{j_}", FW * 2)
        usedC = ar.used()
        print(f"[kernel] arena used: A={usedA} B={usedB} C={usedC} / {AB}")
        ffnh = [V(f"ffnh{j}", BF16, FW) for j in range(24)]
        wgu_off = [ar.blocks["wgu0"][0], ar.blocks["wgu1"][0]]
        v_wgp = [arena[:, wgu_off[p]:wgu_off[p] + 4096]
                 .bitcast(BF16).rearrange("p (d c) -> p d c", d=8)
                 for p in range(2)]
        v_wup = [arena[:, wgu_off[p] + 4096:wgu_off[p] + 8192]
                 .bitcast(BF16).rearrange("p (d c) -> p d c", d=8)
                 for p in range(2)]
        v_wd = [arena[:, wgu_off[p]:wgu_off[p] + 6144]
                .bitcast(BF16).rearrange("p (h c) -> p h c", h=24)
                for p in range(2)]
        wgT_r = i_wgT.rearrange("(d p) h -> p d h", p=128)
        wuT_r = i_wuT.rearrange("(d p) h -> p d h", p=128)
        wdT_r = i_wdT.rearrange("(h p) c -> p h c", p=128)
        v_silb = V("silb", BF16, 2 * FW).rearrange("p (s x) -> p s x", s=2)
        fnu = FW // 512
        wg_rdy_pair = {}
        dve_ffnh = {}
        pe_gu_of = {}

        def load_pair(jj, wait_pe=None, wait_dve=None):
            ps_ = jj % 2
            if wait_pe:
                P.wait("sw", "pe", wait_pe)
            if wait_dve:
                P.wait("sw", "dve", wait_dve)
            P.op("sw", "dma_start", inc="sw", out=v_wgp[ps_],
                 in_=wgT_r[:, :, 256 * jj:256 * jj + 256])
            P.op("sw", "dma_start", inc="sw", out=v_wup[ps_],
                 in_=wuT_r[:, :, 256 * jj:256 * jj + 256])
            wg_rdy_pair[jj] = P.ckpt("sw")

        def emit_gate_up(j, fo, j0_dve=0, act_rows=None, first=False):
            sl = j % 2
            ps_ = (j // 2) % 2
            bank = psA if sl == 0 else psB
            P.wait("pe", "swc", wg_rdy_pair[j // 2])
            if j == 0 or first:
                P.wait("pe", "dve", j0_dve)
            if j >= 2:
                P.wait("pe", "dve", dve_ffnh[j - 2])
            wgv = v_wgp[ps_][:, :, 128 * sl:128 * sl + 128]
            wuv = v_wup[ps_][:, :, 128 * sl:128 * sl + 128]
            for d in range(8):
                if j == 0 and act_rows is not None:
                    P.wait("pe", "act", act_rows[d])
                for u in range(fnu):
                    P.op("pe", "matmul", bank[:, 512 * u:512 * u + 512],
                         wgv[:, d, :],
                         v_gsb[:, d, fo + 512 * u:fo + 512 * u + 512],
                         start=(d == 0), stop=(d == 7),
                         skip_group_check=True)
            for d in range(8):
                for u in range(fnu):
                    P.op("pe", "matmul",
                         bank[:, 1024 + 512 * u:1024 + 512 * u + 512],
                         wuv[:, d, :],
                         v_gsb[:, d, fo + 512 * u:fo + 512 * u + 512],
                         start=(d == 0), stop=(d == 7),
                         skip_group_check=True,
                         inc="pe" if (d == 7 and u == fnu - 1) else None)
            pe_gu = P.n["pe"]
            pe_gu_of[j] = pe_gu
            P.wait("act", "pe", pe_gu)
            P.op("act", "activation", v_silb[:, sl, 0:FW],
                 bank[:, 0:FW], AF.Silu, scale=DEQ)
            a_sil = P.n["act"]
            P.wait("dve", "act", a_sil)
            P.op("dve", "scalar_tensor_tensor", ffnh[j][:, 0:FW],
                 bank[:, 1024:1024 + FW], DEQ, v_silb[:, sl, 0:FW],
                 ALU.mult, ALU.mult)
            dve_ffnh[j] = P.n["dve"]

        if NCHUNK == 1 or NPASS == 1:
            pass
        if not wo_loaded[0]:
            P.op("sw", "dma_start", inc="sw", out=v_wo,
                 in_=i_wo.rearrange("(i p) c -> p i c", p=128))
            wo_loaded[0] = True
        wo_ck = P.ckpt("sw")
        P.wait("hw", "hw", P.n["hw"])
        P.wait("hw", "dve", dve_attn_done)
        P.wait("pe", "act", act_attn_done)
        P.wait("pe", "dve", dve_attn_done)
        prev_win_dve = dve_attn_done
        prev_scale_done = 0
        prev_evac_done = 0
        bank_free = 0
        ln1_state = {}
        prev_gu_pe = 0
        hoisted = False
        ln1_bc0 = 0
        deferred_norm = []
        pe_dn_all = 0
        for hp in range(NPASS):
            lo = LNW * hp
            c0 = lo // HSH
            c1 = (lo + LNW - 1) // HSH
            P.wait("sw", "cc", c1 + 1)
            dve_scale = {}
            if hp in early_grec:
                grec_rdy = early_grec[hp]
            else:
                if hp > 0:
                    P.wait("sw", "dve", prev_scale_done)
                for x in range(c0, c1 + 1):
                    xl = max(lo, HSH * x)
                    xr = min(lo + LNW, HSH * (x + 1))
                    P.op("sw", "dma_start", inc="sw",
                         out=v_grec[:, :, xl - lo:xr - lo],
                         in_=aout[x].rearrange("(i q) t -> q i t", q=130)
                         [128:130, :, xl - HSH * x:xr - HSH * x])
                grec_rdy = P.ckpt("sw")
            if hp in early_gsb:
                gsb_ck = {i: early_gsb[hp][1] for i in range(8)}
            else:
                gsb_ck = {}
                for i in range(8):
                    for x in range(c0, c1 + 1):
                        xl = max(lo, HSH * x)
                        xr = min(lo + LNW, HSH * (x + 1))
                        P.op("sw", "dma_start", inc="sw",
                             out=v_gsb[:, i, xl:xr],
                             in_=aout[x][130 * i:130 * i + 128, xl - HSH * x:
                                         xr - HSH * x])
                    gsb_ck[i] = P.ckpt("sw")
            for i in range(8):
                bank = psA if i % 2 == 0 else psB
                P.wait("pe", "swc", grec_rdy)
                if i >= 2:
                    P.wait("pe", "dve", dve_scale[i - 2])
                elif hp > 0:
                    P.wait("pe", "dve", bank_free)
                for u in range(NU):
                    P.op("pe", "matmul", bank[:, 512 * u:512 * u + 512], v_sel2,
                         v_grec[:, i, 512 * u:512 * u + 512],
                         start=True, stop=True,
                         inc="pe" if u == NU - 1 else None)
                pe_bc = P.n["pe"]
                P.wait("dve", "pe", pe_bc)
                P.wait("dve", "swc", gsb_ck[i])
                P.op("dve", "tensor_tensor", v_gsb[:, i, lo:lo + LNW],
                     v_gsb[:, i, lo:lo + LNW], bank[:, 0:LNW], ALU.mult)
                dve_scale[i] = P.n["dve"]
            prev_scale_done = dve_scale[7]
            pe_scale_all = pe_bc

            if hp > 0:
                P.wait("hw", "dve", prev_evac_done)
            for jj in range(4):
                P.op("hw", "dma_start", inc="hw", out=v_xres[:, jj, 0:LNW],
                     in_=xres_r[:, jj, lo:lo + LNW])
            xres_all = P.ckpt("hw")
            dve_evac = {}
            xres_ck_of = {jj: xres_all for jj in range(4)}
            for jj in range(8):
                if jj >= 4:
                    P.wait("hw", "dve", dve_evac[jj - 4])
                    P.op("hw", "dma_start", inc="hw",
                         out=v_xres[:, jj % 4, 0:LNW],
                         in_=xres_r[:, jj, lo:lo + LNW])
                    xres_ck_of[jj] = P.ckpt("hw")
                bank = psA if jj % 2 == 0 else psB
                if jj == 0:
                    P.wait("pe", "swc", wo_ck)
                P.wait("pe", "dve", dve_scale[7])
                if jj >= 2:
                    P.wait("pe", "dve", dve_evac[jj - 2])
                for i in range(8):
                    for u in range(NU):
                        P.op("pe", "matmul", bank[:, 512 * u:512 * u + 512],
                             v_wo[:, i, 128 * jj:128 * jj + 128],
                             v_gsb[:, i, lo + 512 * u:lo + 512 * u + 512],
                             start=(i == 0), stop=(i == 7),
                             skip_group_check=True,
                             inc="pe" if (i == 7 and u == NU - 1) else None)
                pe_oj = P.n["pe"]
                P.wait("dve", "pe", pe_oj)
                P.wait("dve", "hwc", xres_ck_of[jj])
                P.op("dve", "scalar_tensor_tensor", v_rr[:, jj, lo:lo + LNW],
                     bank[:, 0:LNW], v_bo[:, jj:jj + 1],
                     v_xres[:, jj % 4, 0:LNW], ALU.add, ALU.add)
                dve_evac[jj] = P.n["dve"]
                if deferred_norm:
                    deferred_norm.pop(0)()
            while deferred_norm:
                deferred_norm.pop(0)()
            prev_evac_done = dve_evac[7]
            pe_outproj = P.n["pe"]
            r_done = P.n["dve"]

            bc, slot = emit_ln_stats(lambda j: v_rr[:, j, :], lo, LNW,
                                     pre_dve=r_done,
                                     prev_stats_pe=pe_outproj)
            if hp == 0:
                ln1_rows = {}
                for q_ in range(8):
                    norm_chunk(lambda z: v_rr[:, z, :], lo, LNW, slot, q_,
                               dst_bf_of_j=lambda z: v_gsb[:, z, :])
                    ln1_rows[q_] = P.n["act"]
                ln1_bc0 = bc
                ln1_state[hp] = ("done",)
                if NPASS == 2:
                    load_pair(0, wait_pe=pe_scale_all)
                    emit_gate_up(0, lo, j0_dve=bc, act_rows=ln1_rows)
                    emit_gate_up(1, lo)
                    hoisted = True
                    bank_free = dve_ffnh[1]
                else:
                    bank_free = bc
            else:
                ln1_state[hp] = ("stats", slot, lo)
                bank_free = bc
            prev_win_dve = bc
        pe_outproj_all = P.n["pe"]

        if ln1_state.get(0, (None,))[0] == "done" and not hoisted:
            prev_sp_dve = max(prev_win_dve, ln1_bc0)
        else:
            prev_sp_dve = prev_win_dve
        ffn_hoisted = False
        for hp in range(NPASS):
            if True:
                fo = LNW * hp
                if hp + 1 in ln1_state and ln1_state[hp + 1][0] == "stats":
                    _, nslot, nlo = ln1_state[hp + 1]
                    for q_ in range(8):
                        deferred_norm.append(
                            (lambda q=q_, nslot=nslot, nlo=nlo: norm_chunk(
                                lambda z: v_rr[:, z, :], nlo, LNW, nslot, q,
                                dst_bf_of_j=lambda z: v_gsb[:, z, :])))
                    ln1_state[hp + 1] = ("pending",)

                if hp == 0 and hoisted:
                    j_start = 2
                    load_pair(1, wait_pe=pe_scale_all)
                    load_pair(2, wait_dve=dve_ffnh[1])
                elif hp > 0:
                    j_start = 0
                    load_pair(1, wait_pe=prev_gu_pe)
                else:
                    j_start = 0
                    load_pair(0, wait_pe=pe_scale_all)
                    load_pair(1)
                for j in range(j_start, 24):
                    if deferred_norm and j >= 4:
                        deferred_norm.pop(0)()
                    emit_gate_up(j, fo, j0_dve=prev_sp_dve,
                                 act_rows=(ln1_rows if hp == 0 and
                                           not hoisted and j == 0 else None),
                                 first=(j == j_start))
                    if j % 2 == 1 and j // 2 + 2 < 12:
                        load_pair(j // 2 + 2, wait_dve=dve_ffnh[j])
                while deferred_norm:
                    deferred_norm.pop(0)()
                act_norm_cover = P.n["act"]
                ffnh_done = P.n["dve"]
                prev_gu_pe = P.n["pe"]
                ovl = hp + 1 < NPASS
                ln2s_mm = {}
                ln2_rbf = {}

                def emit_ln2_stats_j(q):
                    P.wait("pe", "dve", ln2_rbf[q][0])
                    P.wait("pe", "act", ln2_rbf[q][1])
                    for u in range(fnu):
                        P.op("pe", "matmul",
                             psB[0:1, 1024 + 512 * u:1024 + 512 * u + 512],
                             v_ones, v_rbf[:, q % 2, 512 * u:512 * u + 512],
                             start=(q == 0), stop=(q == 7),
                             skip_group_check=True)
                        P.op("pe", "matmul",
                             psA[0:1, 1024 + 512 * u:1024 + 512 * u + 512],
                             v_ones, v_sqb[:, q % 2, 512 * u:512 * u + 512],
                             start=(q == 0), stop=(q == 7),
                             skip_group_check=True,
                             inc="pe" if u == fnu - 1 else None)
                    ln2s_mm[q] = P.n["pe"]
                dve_r2 = {}
                pe_dn_of = {}
                for j in range(8):
                    sl = j % 2
                    if j >= 2:
                        P.wait("sw", "dve", dve_r2[j - 2])
                    elif j == 0:
                        P.wait("sw", "pe", pe_gu_of[21])
                    else:
                        P.wait("sw", "pe", prev_gu_pe)
                    P.op("sw", "dma_start", inc="sw", out=v_wd[sl],
                         in_=wdT_r[:, :, 128 * j:128 * j + 128])
                    wd_rdy = P.ckpt("sw")
                    if j == 7 and hp + 1 < NPASS:
                        load_pair(0, wait_pe=pe_dn_of[6])
                    bank = psA if sl == 0 else psB
                    P.wait("pe", "swc", wd_rdy)
                    P.wait("pe", "dve", ffnh_done)
                    if j >= 2:
                        P.wait("pe", "dve", dve_r2[j - 2])
                    for hch in range(24):
                        for u in range(fnu):
                            P.op("pe", "matmul", bank[:, 512 * u:512 * u + 512],
                                 v_wd[sl][:, hch, :],
                                 ffnh[hch][:, 512 * u:512 * u + 512],
                                 start=(hch == 0), stop=(hch == 23),
                                 skip_group_check=True,
                                 inc="pe" if (hch == 23 and u == fnu - 1)
                                 else None)
                    if ovl:
                        emit_ln2_stats_j(j - 2) if j >= 2 else None
                    pe_dn = P.n["pe"]
                    pe_dn_of[j] = pe_dn
                    P.wait("dve", "pe", pe_dn)
                    P.edge("dve")
                    P.op("dve", "scalar_tensor_tensor", v_tmp[0:128, 0:FW],
                         bank[:, 0:FW], DEQ, v_rr[:, j, fo:fo + FW],
                         ALU.mult, ALU.add)
                    P.edge("dve")
                    P.op("dve", "tensor_copy", v_rr[:, j, fo:fo + FW],
                         v_tmp[0:128, 0:FW])
                    dve_r2[j] = P.n["dve"]
                    if ovl:
                        if j >= 2:
                            P.wait("dve", "pe", ln2s_mm[j - 2])
                        P.op("dve", "tensor_copy", v_rbf[:, j % 2, 0:FW],
                             v_rr[:, j, fo:fo + FW])
                        P.wait("act", "dve", dve_r2[j])
                        if j >= 2:
                            P.wait("act", "pe", ln2s_mm[j - 2])
                        P.op("act", "activation", v_sqb[:, j % 2, 0:FW],
                             v_rr[:, j, fo:fo + FW], AF.Square)
                        ln2_rbf[j] = (P.n["dve"], P.n["act"])
                r2_done = P.n["dve"]
                pe_dn_all = P.n["pe"]
                ffn_hoisted = False
                ln2_slot = ln_slot_ctr[0] % 2
                if ovl:
                    ln_slot_ctr[0] += 1
                    emit_ln2_stats_j(6)
                    emit_ln2_stats_j(7)
                    pe_stats = ln2s_mm[7]
                    P.wait("dve", "pe", pe_stats)
                    P.op("dve", "tensor_scalar_mul", v_rowa[:, 0:FW],
                         psB[0:1, 1024:1024 + FW], 1.0 / D)
                    P.edge("dve")
                    P.op("dve", "tensor_tensor", v_rowb[:, 0:FW],
                         v_rowa[:, 0:FW], v_rowa[:, 0:FW], ALU.mult)
                    P.edge("dve")
                    P.op("dve", "scalar_tensor_tensor", v_rowc[:, 0:FW],
                         psA[0:1, 1024:1024 + FW], 1.0 / D, v_rowb[:, 0:FW],
                         ALU.mult, ALU.subtract)
                    dvar = P.n["dve"]
                    P.wait("act", "dve", dvar)
                    P.op("act", "activation", v_rowb[:, 0:FW],
                         v_rowc[:, 0:FW], AF.Ln, bias=v_eps[:])
                    P.edge("act")
                    mrow = v_rowbf[:, 0:FW]
                    rrow = v_rowbf[:, LNW:LNW + FW]
                    P.op("act", "activation", rrow, v_rowb[:, 0:FW],
                         AF.Exp, scale=-0.5)
                    P.op("act", "activation", mrow, v_rowa[:, 0:FW], AF.Copy)
                    a_rows = P.n["act"]
                    P.wait("pe", "act", a_rows)
                    for u in range(FW // 512):
                        P.op("pe", "matmul", psA[:, 512 * u:512 * u + 512],
                             v_onesr, mrow[:, 512 * u:512 * u + 512],
                             start=True, stop=True)
                        P.op("pe", "matmul", psB[:, 512 * u:512 * u + 512],
                             v_onesr, rrow[:, 512 * u:512 * u + 512],
                             start=True, stop=True,
                             inc="pe" if u == FW // 512 - 1 else None)
                    pe_bcast = P.n["pe"]
                    P.wait("dve", "pe", pe_bcast)
                    P.op("dve", "tensor_copy",
                         v_mbc[:, LNW * ln2_slot:LNW * ln2_slot + FW],
                         psA[:, 0:FW])
                    P.op("dve", "tensor_copy",
                         v_rbc[:, LNW * ln2_slot:LNW * ln2_slot + FW],
                         psB[:, 0:FW])
                    ln2_bc = P.n["dve"]
                else:
                    ln2_bc, ln2_slot = emit_ln_stats(
                        lambda j: v_rr[:, j, :], fo, FW,
                        pre_dve=r2_done, prev_stats_pe=pe_dn_all)
                if hp + 1 < NPASS:
                    for q_ in range(8):
                        deferred_norm.append(
                            (lambda q=q_, sl_=ln2_slot, fo_=fo: norm_chunk(
                                lambda z: v_rr[:, z, :], fo_, FW, sl_, q,
                                out_dma_of_j=lambda z:
                                outr[:, z, fo_:fo_ + FW])))
                else:
                    emit_ln_norm(lambda j: v_rr[:, j, :], fo, FW, ln2_slot,
                                 out_dma_of_j=lambda j: outr[:, j, fo:fo + FW])
                prev_sp_dve = ln2_bc
                prev_gu_pe = pe_dn_all

        out_ckpt = P.ckpt("hw")
        P.wait("sw", "hwc", out_ckpt)
        P.wait("sw", "sw", P.n["sw"])

        if not dense_sync:
            P.finalize_device()
        nops = {k: len(P.q[k]) for k in ENGS}
        nincs = {k: sum(1 for it in P.q[k] if it[0] == "o" and it[4])
                 for k in ENGS}
        print(f"[kernel] plan ops: {nops}; incs: {nincs}; counters {P.n}")

        def runner(name):
            def run(h):
                for item in P.q[name]:
                    if item[0] == "w":
                        _, sname, val = item
                        h.wait_ge(sem[sname], val)
                    elif item[0] == "o":
                        _, method, args, kw, inc = item
                        ins = getattr(h, method)(*args, **kw)
                        if inc is not None:
                            ins.then_inc(sem[inc],
                                         16 if inc in ("hw", "sw") else 1)
                    else:
                        _, fn, inc = item
                        ins = fn(h)
                        ins.then_inc(sem[inc])
            return run

        block = e(nc.Block())
        block.tensor(runner("pe"))
        block.scalar(runner("act"))
        block.vector(runner("dve"))
        block.sync(runner("hw"))
        block.gpsimd(runner("sw"))

    return nc


def stage_inputs(inputs, T):
    x = np.asarray(inputs["L0_attn_x"], np.float32)
    wq = np.asarray(inputs["L0_attn_wq"], np.float32)
    bq = np.asarray(inputs["L0_attn_bq"], np.float32)
    wk = np.asarray(inputs["L0_attn_wk"], np.float32)
    bk = np.asarray(inputs["L0_attn_bk"], np.float32)
    wv = np.asarray(inputs["L0_attn_wv"], np.float32)
    bv = np.asarray(inputs["L0_attn_bv"], np.float32)
    wo = np.ascontiguousarray(np.asarray(inputs["L0_attn_wo"], np.float32))
    bo = np.asarray(inputs["L0_attn_bo"], np.float32)
    wg = np.asarray(inputs["L0_ffn_w_gate"], np.int32)
    wu = np.asarray(inputs["L0_ffn_w_up"], np.int32)
    wd = np.asarray(inputs["L0_ffn_w_down"], np.int32)

    TOKC = 2 * T // NCORE
    xT = [np.ascontiguousarray(x[b].T) for b in range(2)]
    xS = [np.ascontiguousarray(
        xT[b].reshape(8, 128, T // 1024, 1024)
        .transpose(1, 2, 0, 3).reshape(128, -1)) for b in range(2)]
    wgT = np.ascontiguousarray(wg.T)
    wuT = np.ascontiguousarray(wu.T)
    wdT = np.ascontiguousarray(wd.T)
    bo_c = np.ascontiguousarray(bo.reshape(8, 128).T)

    def qkv_s(w, hsl):
        return np.ascontiguousarray(
            w[:, hsl].reshape(8, 128, 128).transpose(1, 0, 2).reshape(128, -1))

    masks = np.zeros((512, 1024), ml_dtypes.bfloat16)
    for j in range(4):
        k = np.arange(128)[:, None]
        q = np.arange(512)[None, :]
        m = (q >= 128 * j + k).astype(ml_dtypes.bfloat16)
        masks[128 * j:128 * j + 128, 0:512] = m
        masks[128 * j:128 * j + 128, 512:1024] = m
    ones = np.ones((128, 1), np.float32)
    onesr = np.ones((1, 128), np.float32)
    sel2 = np.zeros((2, 128), np.float32)
    sel2[0, 0:64] = 1.0
    sel2[1, 64:128] = 1.0

    in_maps = []
    for c in range(NCORE):
        b = (c * TOKC) // T
        t0 = (c * TOKC) % T
        hsl = slice(128 * c, 128 * c + 128)
        in_maps.append({
            "xT0": xS[0], "xT1": xS[1],
            "xres": np.ascontiguousarray(xT[b][:, t0:t0 + TOKC]),
            "wq": qkv_s(wq, hsl),
            "wk": qkv_s(wk, hsl),
            "wv": qkv_s(wv, hsl),
            "bq": np.ascontiguousarray(bq[hsl].reshape(128, 1)),
            "bk": np.ascontiguousarray(bk[hsl].reshape(128, 1)),
            "bvrow": np.ascontiguousarray(bv[hsl].reshape(1, 128)),
            "wo": wo, "bo": bo_c,
            "wgT": wgT, "wuT": wuT, "wdT": wdT,
            "masks": masks, "onescol": ones, "onesrow": onesr, "sel2": sel2,
        })
    return in_maps


def assemble_output(results, T):
    TOKC = 2 * T // NCORE
    out = np.empty((2, T, D), np.float32)
    for c in range(NCORE):
        b = (c * TOKC) // T
        t0 = (c * TOKC) % T
        out[b, t0:t0 + TOKC, :] = np.asarray(results[c]["outT"]).T
    return out


def kernel(**inputs):
    T = int(np.asarray(inputs["L0_attn_x"]).shape[1])
    in_maps = stage_inputs(inputs, T)
    nc = build_nc(T)
    res = run_bass_kernel_spmd(nc, in_maps, core_ids=list(range(NCORE)))
    return assemble_output(res.results, T)


# version 42
# speedup vs baseline: 1.0002x; 1.0002x over previous
import numpy as np
import ml_dtypes
import concourse.bass as bass
import concourse.mybir as mybir
from concourse.bass_utils import run_bass_kernel_spmd
from concourse.aot_env import allow_ici_all_to_all
from contextlib import ExitStack

F32 = mybir.dt.float32
BF16 = mybir.dt.bfloat16
I32 = mybir.dt.int32
U8 = mybir.dt.uint8
AF = mybir.ActivationFunctionType
ALU = mybir.AluOpType
X = mybir.AxisListType.X

D = 1024
HID = 3072
NCORE = 8
SCALE = 0.125
DEQ = 1.0 / 127.0
ENGS = ("pe", "act", "dve", "hw", "sw")


class Arena:
    def __init__(self, size):
        self.size = size
        self.blocks = {}
        self.free_list = [(0, size)]

    def alloc(self, name, nbytes, align=32):
        assert name not in self.blocks
        nbytes = (nbytes + align - 1) // align * align
        for i, (off, sz) in enumerate(self.free_list):
            if nbytes <= sz:
                repl = [(off + nbytes, sz - nbytes)] if sz > nbytes else []
                self.free_list[i:i + 1] = repl
                self.blocks[name] = (off, nbytes)
                return off
        raise MemoryError(f"arena: no room for {name} ({nbytes}B); free={self.free_list}")

    def free(self, name):
        off, sz = self.blocks.pop(name)
        self.free_list.append((off, sz))
        self.free_list.sort()
        out = []
        for b in self.free_list:
            if out and out[-1][0] + out[-1][1] == b[0]:
                out[-1] = (out[-1][0], out[-1][1] + b[1])
            else:
                out.append(b)
        self.free_list = out

    def used(self):
        return self.size - sum(s for _, s in self.free_list)


class Plan:

    def __init__(self):
        self.q = {e: [] for e in ENGS}
        self.n = dict(pe=0, act=0, dve=0, hw=0, sw=0, cc=0, hwc=0, swc=0)

    def wait(self, eng, sname, val):
        if val <= 0:
            return
        if sname in ("hw", "sw"):
            val *= 16
        self.q[eng].append(("w", sname, val))

    def finalize_device(self):
        waited = {}
        for e in ENGS:
            for it in self.q[e]:
                if it[0] == "w":
                    waited.setdefault(it[1], set()).add(it[2])
        remap = {}
        for e in ("dve", "act"):
            w = waited.get(e, set())
            old = new = 0
            m = {0: 0}
            outq = []
            for it in self.q[e]:
                if it[0] == "o" and it[4] == e:
                    old += 1
                    if old in w:
                        new += 1
                        outq.append(it)
                    else:
                        outq.append((it[0], it[1], it[2], it[3], None))
                    m[old] = new
                else:
                    outq.append(it)
            self.q[e] = outq
            remap[e] = m
        for e in ENGS:
            outq = []
            for it in self.q[e]:
                if it[0] == "w" and it[1] in remap:
                    outq.append(("w", it[1], remap[it[1]][it[2]]))
                else:
                    outq.append(it)
            self.q[e] = outq

    def edge(self, eng):
        self.wait(eng, eng, self.n[eng])

    def ckpt(self, q):
        self.wait(q, q, self.n[q])
        self.n[q + "c"] += 1
        self.q[q].append(("o", "nop", (), {}, q + "c"))
        return self.n[q + "c"]

    def op(self, eng, method, *args, inc=None, **kw):
        if eng in ("dve", "act"):
            inc = eng
        if inc is not None:
            self.n[inc] += 1
        self.q[eng].append(("o", method, args, kw, inc))


def build_nc(T, dense_sync=False):
    assert T % 1024 == 0
    TOKG = 2 * T
    TOKC = TOKG // NCORE
    NT = T // 512
    NSC = T // 1024
    NKB = T // 128
    SHARD = TOKC
    assert SHARD % 512 == 0
    NTC = TOKC // 512
    LNW = min(1024, TOKC)
    NPASS = TOKC // LNW
    NU = LNW // 512
    NCH = 2 * NSC

    nc = bass.Bass("TRN2", target_bir_lowering=False)

    i_xT = [nc.dram_tensor(f"xT{b}", [128, 8 * T], F32,
                           kind="ExternalInput").ap() for b in range(2)]
    i_xres = nc.dram_tensor("xres", [D, TOKC], F32, kind="ExternalInput").ap()
    i_wq = nc.dram_tensor("wq", [128, D], F32, kind="ExternalInput").ap()
    i_wk = nc.dram_tensor("wk", [128, D], F32, kind="ExternalInput").ap()
    i_wv = nc.dram_tensor("wv", [128, D], F32, kind="ExternalInput").ap()
    i_bq = nc.dram_tensor("bq", [128, 1], F32, kind="ExternalInput").ap()
    i_bk = nc.dram_tensor("bk", [128, 1], F32, kind="ExternalInput").ap()
    i_bv = nc.dram_tensor("bvrow", [1, 128], F32, kind="ExternalInput").ap()
    i_wo = nc.dram_tensor("wo", [D, D], F32, kind="ExternalInput").ap()
    i_bo = nc.dram_tensor("bo", [128, 8], F32, kind="ExternalInput").ap()
    i_wgT = nc.dram_tensor("wgT", [D, HID], I32, kind="ExternalInput").ap()
    i_wuT = nc.dram_tensor("wuT", [D, HID], I32, kind="ExternalInput").ap()
    i_wdT = nc.dram_tensor("wdT", [HID, D], I32, kind="ExternalInput").ap()
    i_masks = nc.dram_tensor("masks", [512, 1024], BF16, kind="ExternalInput").ap()
    i_ones = nc.dram_tensor("onescol", [128, 1], F32, kind="ExternalInput").ap()
    i_onesr = nc.dram_tensor("onesrow", [1, 128], F32, kind="ExternalInput").ap()
    i_sel2 = nc.dram_tensor("sel2", [2, 128], F32, kind="ExternalInput").ap()
    o_out = nc.dram_tensor("outT", [D, TOKC], F32, kind="ExternalOutput").ap()
    NCHUNK = 2 if SHARD >= 1024 else 1
    HSH = SHARD // NCHUNK
    d_ain = [nc.dram_tensor(f"a2a_in{x}", [8 * 130, HSH], BF16, kind="Internal")
             for x in range(NCHUNK)]
    d_aout = [nc.dram_tensor(f"a2a_out{x}", [8 * 130, HSH], BF16, kind="Internal")
              for x in range(NCHUNK)]

    AB = 207 * 1024
    ar = Arena(AB)
    P = Plan()

    ctx = ExitStack()
    with ctx:
        e = ctx.enter_context
        e(nc.allow_low_precision(
            reason="bf16 matmul operands with f32 accumulation; explicitly "
                   "sanctioned by this problem's stated tolerance"))
        arena = e(nc.sbuf_tensor("arena", [128, AB], U8))
        psA = e(nc.psum_tensor("psA", [128, 2048], F32))
        psB = e(nc.psum_tensor("psB", [128, 2048], F32))
        sem = {k: e(nc.semaphore(f"s_{k}"))
               for k in ("pe", "act", "dve", "hw", "sw", "cc", "hwc", "swc")}

        def V(name, dtype, cols, parts=128, p0=0):
            off, nb = ar.blocks[name]
            esz = mybir.dt.size(dtype)
            assert cols * esz <= nb, (name, cols * esz, nb)
            return arena[p0:p0 + parts, off:off + cols * esz].bitcast(dtype)

        for nm, nb in (("bqkv", 12), ("onescol", 2), ("biasc", 8),
                       ("bvb", 512), ("bvb4", 2048), ("bo", 32), ("sel2", 256), ("onesrow", 256),
                       ("bvrow", 256), ("eps", 4), ("ntmp", 8 * 16 * 4),
                       ("scr", 64), ("wqb", 2048), ("wkb", 2048), ("wvb", 2048)):
            ar.alloc(nm, nb)
        TF = max(T, 8192)
        for b in range(2):
            ar.alloc(f"qt{b}", TF * 2)
            ar.alloc(f"kt{b}", TF * 2)
        NUNIT = 2 * NKB * 2
        ar.alloc("vsb", 2 * (TF // 128) * 2 * 65 * 2)
        ar.alloc("xt0", 16384)
        ar.alloc("xt1", 16384)
        ar.alloc("xt2", 16384)
        ar.alloc("pbuf", 8192)
        ar.alloc("ptmp", 4096)
        ar.alloc("sq0", 1024)
        ar.alloc("sq1", 1024)
        ar.alloc("maskt", 8192)
        ar.alloc("hobuf", 2048)
        ar.alloc("recf", 4096)
        ar.alloc("recb", 4096)
        ar.alloc("wo", 16384)
        usedA = ar.used()

        v_ones = V("onescol", BF16, 1)
        v_onesr = V("onesrow", BF16, 128, parts=1)
        v_sel2 = V("sel2", BF16, 128, parts=2)
        v_bvrow = V("bvrow", BF16, 128, parts=1)
        v_bvb = V("bvb", F32, 128)
        v_bvb4 = V("bvb4", F32, 512).rearrange("p (r c) -> p r c", r=4)
        v_bqkv = V("bqkv", F32, 3)
        v_biasc = V("biasc", F32, 2)
        v_bo = V("bo", F32, 8)
        v_eps = V("eps", F32, 1, parts=1)
        v_ntmp = V("ntmp", F32, 8 * NCH, parts=1)
        v_scr = V("scr", F32, 16, parts=1)
        v_wq = V("wqb", BF16, 1024).rearrange("p (d c) -> p d c", d=8)
        v_wk = V("wkb", BF16, 1024).rearrange("p (d c) -> p d c", d=8)
        v_wv = V("wvb", BF16, 1024).rearrange("p (d c) -> p d c", d=8)
        v_qt = [V(f"qt{b}", BF16, T) for b in range(2)]
        v_kt = [V(f"kt{b}", BF16, T) for b in range(2)]
        v_vsb = V("vsb", BF16, NUNIT * 65).rearrange("p (u c) -> p u c", u=NUNIT)
        v_xt = [V("xt0", BF16, 8192).rearrange("p (d t) -> p d t", d=8),
                V("xt1", BF16, 8192).rearrange("p (d t) -> p d t", d=8),
                V("xt2", BF16, 8192).rearrange("p (d t) -> p d t", d=8)]
        v_xt_flat = [V("xt0", BF16, 8192), V("xt1", BF16, 8192),
                     V("xt2", BF16, 8192)]
        v_pbuf = V("pbuf", BF16, 4096)
        v_ptmp = V("ptmp", BF16, 2048)
        v_sq = [V("sq0", BF16, 512), V("sq1", BF16, 512)]
        v_maskt = V("maskt", BF16, 4096).rearrange("p (j q) -> p j q", j=4)
        v_hob = V("hobuf", BF16, 1024)
        v_recf = V("recf", F32, 1024, parts=1)
        v_recb = V("recb", BF16, 2048, parts=1)
        v_wo = V("wo", BF16, 8192).rearrange("p (i c) -> p i c", i=8)

        P.op("hw", "dma_start", inc="hw",
             out=V("maskt", BF16, 4096).rearrange("p (j q) -> p j q", j=4),
             in_=i_masks.rearrange("(j p) q -> p j q", p=128))
        P.op("hw", "dma_start", inc="hw", out=v_bqkv[:, 0:1], in_=i_bq)
        P.op("hw", "dma_start", inc="hw", out=v_bqkv[:, 1:2], in_=i_bk)
        P.op("hw", "dma_start", inc="hw", out=v_bo, in_=i_bo)
        hw_consts = P.ckpt("hw")
        sc_list = [(b, sc) for b in range(2) for sc in range(NSC)]
        xt_ready = {}

        def emit_xt_load(idx, wait_pe):
            b, sc = sc_list[idx]
            P.wait("sw", "pe", wait_pe)
            P.op("sw", "dma_start", inc="sw", out=v_xt_flat[idx % 3],
                 in_=i_xT[b][:, 8192 * sc:8192 * sc + 8192])
            ck = P.ckpt("sw")
            xt_ready[idx] = [ck, ck]

        P.op("sw", "dma_start", inc="sw", out=v_xt_flat[0],
             in_=i_xT[0][:, 0:8192])
        P.op("sw", "dma_start", inc="sw", out=V("wqb", BF16, 1024), in_=i_wq)
        P.op("sw", "dma_start", inc="sw", out=V("wkb", BF16, 1024), in_=i_wk)
        P.op("sw", "dma_start", inc="sw", out=V("wvb", BF16, 1024), in_=i_wv)
        ck0 = P.ckpt("sw")
        xt_ready[0] = [ck0, ck0]
        P.op("sw", "dma_start", inc="sw", out=v_ones, in_=i_ones)
        P.op("sw", "dma_start", inc="sw", out=v_onesr, in_=i_onesr)
        P.op("sw", "dma_start", inc="sw", out=v_sel2, in_=i_sel2)
        P.op("sw", "dma_start", inc="sw", out=v_bvrow, in_=i_bv)
        sw_consts = P.ckpt("sw")

        P.wait("dve", "hwc", hw_consts)
        P.wait("act", "hwc", hw_consts)
        P.op("dve", "memset", v_eps[:], 1e-5)
        P.op("dve", "memset", v_ptmp[:], 0.0)
        P.op("dve", "memset", v_vsb[:, :, 64:65], 1.0)
        P.wait("pe", "swc", sw_consts)
        P.op("pe", "matmul", psB[:, 0:128], v_onesr, v_bvrow,
             start=True, stop=True, inc="pe")
        P.wait("dve", "pe", P.n["pe"])
        P.op("dve", "tensor_copy", v_bvb, psB[:, 0:128])
        P.edge("dve")
        for r_ in range(4):
            P.op("dve", "tensor_copy", v_bvb4[:, r_, :], v_bvb)
        dve_bvb = P.n["dve"]

        if len(sc_list) > 1:
            emit_xt_load(1, 0)
        if len(sc_list) > 2:
            emit_xt_load(2, 0)

        wo_loaded = [False]
        prev_qk_act = 0
        prev_v_ev = {0: dve_bvb, 1: dve_bvb}
        prev_norm_ev = 0
        for idx, (b, sc) in enumerate(sc_list):
            slot = idx % 3
            pe_v_last = 0
            for tc2 in range(2):
                P.wait("pe", "swc", xt_ready[idx][tc2])
                tt = 1024 * sc + 512 * tc2
                tl = 512 * tc2
                P.wait("pe", "act", prev_qk_act)
                for d in range(8):
                    P.op("pe", "matmul", psA[:, 0:512], v_wq[:, d, :],
                         v_xt[slot][:, d, tl:tl + 512],
                         start=(d == 0), stop=(d == 7),
                         inc="pe" if d == 7 else None)
                pe_q = P.n["pe"]
                for d in range(8):
                    P.op("pe", "matmul", psA[:, 512:1024], v_wk[:, d, :],
                         v_xt[slot][:, d, tl:tl + 512],
                         start=(d == 0), stop=(d == 7),
                         inc="pe" if d == 7 else None)
                pe_k = P.n["pe"]
                P.wait("act", "pe", pe_q)
                P.op("act", "activation", v_sq[0][:], psA[:, 0:512], AF.Square,
                     bias=v_bqkv[:, 0:1])
                P.op("act", "add", v_qt[b][:, tt:tt + 512],
                     psA[:, 0:512], v_bqkv[:, 0:1])
                act_q = P.n["act"]
                P.wait("act", "pe", pe_k)
                P.op("act", "activation", v_sq[1][:], psA[:, 512:1024], AF.Square,
                     bias=v_bqkv[:, 1:2])
                P.op("act", "add", v_kt[b][:, tt:tt + 512],
                     psA[:, 512:1024], v_bqkv[:, 1:2])
                act_k = P.n["act"]
                prev_qk_act = act_k
                vb = tc2
                P.wait("pe", "dve", prev_v_ev[vb])
                for tb in range(4):
                    for d in range(8):
                        P.op("pe", "matmul",
                             psB[:, 512 * vb + 128 * tb:512 * vb + 128 * tb + 128],
                             v_xt[slot][:, d, tl + 128 * tb:tl + 128 * tb + 128],
                             v_wv[:, d, :],
                             start=(tb == 0 and d == 0),
                             stop=(tb == 3 and d == 7), skip_group_check=True,
                             inc="pe" if (tb == 3 and d == 7) else None)
                pe_v = P.n["pe"]
                pe_v_last = pe_v
                P.wait("pe", "act", act_k)
                P.wait("pe", "dve", prev_norm_ev)
                ndst = [psA[0:1, 1024:1536], psA[0:1, 1536:2048],
                        psB[0:1, 1024:1536], psB[0:1, 1536:2048]]
                for v in range(4):
                    half, sq = v % 2, v // 2
                    P.op("pe", "matmul", ndst[v],
                         v_ones[64 * half:64 * half + 64, :],
                         v_sq[sq][64 * half:64 * half + 64, :],
                         start=True, stop=True,
                         tile_position=(64 * half, 0),
                         inc="pe" if v == 3 else None)
                pe_norm = P.n["pe"]
                P.wait("dve", "pe", pe_norm)
                ch = sc * 2 + tc2
                for v in range(4):
                    dst = (b * 4 + v) * NCH + ch
                    P.op("dve", "reduce_max", v_ntmp[:, dst:dst + 1],
                         ndst[v], axis=X)
                prev_norm_ev = P.n["dve"]
                P.wait("dve", "pe", pe_v)
                kb0 = tt // 128
                u0 = (b * NKB + kb0) * 2
                for hh in range(2):
                    P.op("dve", "tensor_tensor",
                         v_vsb[:, u0 + hh:u0 + hh + 7:2, 0:64],
                         psB[:, 512 * vb:512 * vb + 512].rearrange(
                             "p (tb c) -> p tb c", tb=4)[:, :, 64 * hh:
                                                         64 * hh + 64],
                         v_bvb4[:, :, 64 * hh:64 * hh + 64], ALU.add)
                prev_v_ev[vb] = P.n["dve"]
            if idx + 3 < len(sc_list):
                emit_xt_load(idx + 3, pe_v_last)

        P.edge("dve")
        for b in range(2):
            for v in range(4):
                base = (b * 4 + v) * NCH
                P.op("dve", "tensor_reduce", v_scr[:, v:v + 1],
                     v_ntmp[:, base:base + NCH], X, ALU.max)
            P.edge("dve")
            P.op("dve", "tensor_tensor", v_scr[:, 4:5], v_scr[:, 0:1],
                 v_scr[:, 2:3], ALU.mult)
            P.op("dve", "tensor_tensor", v_scr[:, 5:6], v_scr[:, 1:2],
                 v_scr[:, 3:4], ALU.mult)
            P.edge("dve")
            P.op("dve", "tensor_tensor", v_scr[:, 6:7], v_scr[:, 4:5],
                 v_scr[:, 5:6], ALU.max)
            P.wait("act", "dve", P.n["dve"])
            P.op("act", "activation", v_scr[:, 7:8], v_scr[:, 6:7], AF.Sqrt)
            P.wait("dve", "act", P.n["act"])
            P.op("dve", "tensor_scalar_mul", v_scr[:, 7:8], v_scr[:, 7:8],
                 -(SCALE * 1.0103))
            P.edge("dve")
            vb16 = v_scr[:, 12:13].bitcast(BF16)[:, 0:1]
            P.op("dve", "tensor_copy", vb16, v_scr[:, 7:8])
            P.wait("pe", "dve", P.n["dve"])
            P.op("pe", "matmul", psB[:, 128:129], v_onesr, vb16,
                 start=True, stop=True, inc="pe")
            P.wait("dve", "pe", P.n["pe"])
            P.op("dve", "tensor_copy", v_biasc[:, b:b + 1], psB[:, 128:129])
        dve_bias = P.n["dve"]

        xt0_off = ar.blocks["xt0"][0]
        ar.free("xt0")
        ar.free("xt1")
        ar.free("xt2")
        gsb_off = ar.alloc("gsb", 8 * 4096)
        assert gsb_off == xt0_off, (gsb_off, xt0_off)
        ar.alloc("grecip", 8 * LNW * 2)
        v_gsb_early = V("gsb", BF16, 8 * 2048).rearrange("p (i t) -> p i t", i=8)
        v_grec_early = V("grecip", BF16, 8 * LNW,
                         parts=2).rearrange("p (i t) -> p i t", i=8)
        early_gsb = {}
        early_grec = {}
        P.wait("act", "dve", dve_bias)
        P.wait("pe", "dve", dve_bias)
        g = 0
        act_of_g = {}
        mask_of_g = {}
        ptmp_user = {}
        tile_ep = {}
        hw_tile = {}
        tidx = 0
        ain = [x.ap() for x in d_ain]

        def chunk_of(b, t):
            return ((b * T + 512 * t) % SHARD) // HSH
        tile_list = [(b, t) for c in range(NCHUNK)
                     for b in range(2) for t in range(NT) if chunk_of(b, t) == c]
        n_tiles_A = (sum(1 for b, t in tile_list if chunk_of(b, t) == 0)
                     if NCHUNK == 2 else -1)
        pending_tail = [None]
        for ti, (b, t) in enumerate(tile_list):
            if True:
                nkb = 4 * t + 4
                bko = 1024 * (ti % 2)
                gbase = g

                def emit_av(pkb, gbase=gbase, nkb=nkb, bko=bko, b=b, ti=ti):
                    psl = (gbase + pkb) % 4
                    if pkb == 0:
                        P.wait("pe", "dve", tile_ep.get(ti - 2, 0))
                    gg = gbase + pkb
                    if gg in mask_of_g:
                        P.wait("pe", "dve", mask_of_g[gg])
                    else:
                        P.wait("pe", "act", act_of_g[gg])
                    for hh in range(2):
                        u = (b * NKB + pkb) * 2 + hh
                        P.op("pe", "matmul",
                             psB[0:65, bko + 512 * hh:bko + 512 * hh + 512],
                             v_vsb[:, u, :],
                             v_pbuf[:, 1024 * psl + 512 * hh:
                                    1024 * psl + 512 * hh + 512],
                             start=(pkb == 0), stop=(pkb == nkb - 1),
                             skip_group_check=True,
                             inc="pe" if (hh == 1 and pkb == nkb - 1) else None)

                for kb in range(nkb):
                    sl2 = g % 2
                    sl3 = g % 4
                    diag = kb >= 4 * t
                    j = kb - 4 * t
                    if g >= 2:
                        P.wait("pe", "act", act_of_g[g - 2])
                    P.op("pe", "matmul", psA[:, 1024 * sl2:1024 * sl2 + 512],
                         v_kt[b][0:64, 128 * kb:128 * kb + 128],
                         v_qt[b][0:64, 512 * t:512 * t + 512],
                         start=True, stop=True, tile_position=(0, 0))
                    P.op("pe", "matmul", psA[:, 1024 * sl2 + 512:1024 * sl2 + 1024],
                         v_kt[b][64:128, 128 * kb:128 * kb + 128],
                         v_qt[b][64:128, 512 * t:512 * t + 512],
                         start=True, stop=True, tile_position=(64, 0), inc="pe")
                    pe_s = P.n["pe"]
                    P.wait("act", "pe", pe_s)
                    if diag:
                        pts = g % 2
                        P.wait("act", "dve", ptmp_user.get(pts, 0))
                        if j >= 2:
                            c = 128 * j
                            for hh in range(2):
                                P.op("act", "activation",
                                     v_ptmp[:, 1024 * pts + 512 * hh + c:
                                            1024 * pts + 512 * hh + 512],
                                     psA[:, 1024 * sl2 + 512 * hh + c:
                                         1024 * sl2 + 512 * hh + 512],
                                     AF.Exp, bias=v_biasc[:, b:b + 1],
                                     scale=SCALE)
                        else:
                            P.op("act", "activation",
                                 v_ptmp[:, 1024 * pts:1024 * pts + 1024],
                                 psA[:, 1024 * sl2:1024 * sl2 + 1024], AF.Exp,
                                 bias=v_biasc[:, b:b + 1], scale=SCALE)
                    else:
                        P.op("act", "activation",
                             v_pbuf[:, 1024 * sl3:1024 * sl3 + 1024],
                             psA[:, 1024 * sl2:1024 * sl2 + 1024], AF.Exp,
                             bias=v_biasc[:, b:b + 1], scale=SCALE)
                    act_of_g[g] = P.n["act"]
                    if diag:
                        P.wait("dve", "act", act_of_g[g])
                        P.op("dve", "tensor_tensor",
                             v_pbuf[:, 1024 * sl3:1024 * sl3 + 1024],
                             v_ptmp[:, 1024 * pts:1024 * pts + 1024],
                             v_maskt[:, j, :], ALU.mult)
                        mask_of_g[g] = P.n["dve"]
                        ptmp_user[pts] = P.n["dve"]
                    if kb >= 2:
                        emit_av(kb - 2)
                    g += 1
                    if kb == 1 and pending_tail[0] is not None:
                        pending_tail[0]()
                        pending_tail[0] = None

                def tile_tail(emit_av=emit_av, nkb=nkb, bko=bko, b=b, t=t,
                              ti=ti):
                    emit_av(nkb - 2)
                    emit_av(nkb - 1)
                    pe_av = P.n["pe"]
                    hsl = ti % 2
                    last_tile = (ti == len(tile_list) - 1)
                    P.wait("dve", "pe", pe_av)
                    if ti >= 2:
                        P.wait("dve", "hwc", hw_tile[ti - 2])
                    P.edge("dve")
                    if last_tile:
                        P.wait("act", "pe", pe_av)
                        if ti >= 1:
                            P.wait("act", "dve", tile_ep[ti - 1])
                        if ti >= 2:
                            P.wait("act", "hwc", hw_tile[ti - 2])
                        P.op("act", "activation", v_recf[:, 0:1024],
                             psB[64:65, bko:bko + 1024], AF.Ln)
                        P.edge("act")
                        P.op("act", "activation",
                             v_recb[:, 1024 * hsl:1024 * hsl + 1024],
                             v_recf[:, 0:1024], AF.Exp, scale=-1.0)
                    else:
                        P.op("dve", "reciprocal", v_recf[:, 0:512],
                             psB[64:65, bko:bko + 512])
                        P.edge("dve")
                        P.op("dve", "tensor_copy",
                             v_recb[:, 1024 * hsl:1024 * hsl + 512],
                             v_recf[:, 0:512])
                        P.op("dve", "reciprocal", v_recf[:, 512:1024],
                             psB[64:65, bko + 512:bko + 1024])
                        P.edge("dve")
                        P.op("dve", "tensor_copy",
                             v_recb[:, 1024 * hsl + 512:1024 * hsl + 1024],
                             v_recf[:, 512:1024])
                    P.op("dve", "tensor_copy",
                         v_hob[0:64, 512 * hsl:512 * hsl + 512],
                         psB[0:64, bko:bko + 512])
                    P.op("dve", "tensor_copy",
                         v_hob[64:128, 512 * hsl:512 * hsl + 512],
                         psB[0:64, bko + 512:bko + 1024])
                    tile_ep[ti] = P.n["dve"]
                    gt0 = b * T + 512 * t
                    sh = gt0 // SHARD
                    ch = chunk_of(b, t)
                    c0 = (gt0 % SHARD) % HSH
                    P.wait("hw", "dve", tile_ep[ti])
                    if last_tile:
                        P.wait("hw", "act", P.n["act"])
                    P.op("hw", "dma_start", inc="hw",
                         out=ain[ch][130 * sh:130 * sh + 128, c0:c0 + 512],
                         in_=v_hob[:, 512 * hsl:512 * hsl + 512])
                    P.op("hw", "dma_start", inc="hw",
                         out=ain[ch][130 * sh + 128:130 * sh + 129, c0:c0 + 512],
                         in_=v_recb[:, 1024 * hsl:1024 * hsl + 512])
                    P.op("hw", "dma_start", inc="hw",
                         out=ain[ch][130 * sh + 129:130 * sh + 130, c0:c0 + 512],
                         in_=v_recb[:, 1024 * hsl + 512:1024 * hsl + 1024])
                    hw_tile[ti] = P.ckpt("hw")
                    if ti + 1 == n_tiles_A:
                        P.wait("sw", "hwc", hw_tile[ti])

                        def _ccA(h):
                            with allow_ici_all_to_all():
                                return h.collective_compute(
                                    "AllToAll", ALU.bypass,
                                    replica_groups=[list(range(NCORE))],
                                    ins=[d_ain[0].ap()], outs=[d_aout[0].ap()])
                        P.q["sw"].append(("raw", _ccA, "cc"))
                        P.n["cc"] += 1
                        P.op("sw", "dma_start", inc="sw", out=v_wo,
                             in_=i_wo.rearrange("(i p) c -> p i c", p=128))
                        wo_loaded[0] = True
                        if NCHUNK == 2 and NPASS == 2:
                            P.wait("sw", "cc", 1)
                            for i_ in range(8):
                                P.op("sw", "dma_start", inc="sw",
                                     out=v_gsb_early[:, i_, 0:HSH],
                                     in_=d_aout[0].ap()
                                     [130 * i_:130 * i_ + 128, :])
                            early_gsb[0] = (0, P.ckpt("sw"))
                            P.op("sw", "dma_start", inc="sw",
                                 out=v_grec_early[:, :, 0:HSH],
                                 in_=d_aout[0].ap().rearrange(
                                     "(i q) t -> q i t", q=130)[128:130, :, :])
                            early_grec[0] = P.ckpt("sw")
                pending_tail[0] = tile_tail
        pending_tail[0]()
        pending_tail[0] = None
        dve_tilefree = tile_ep[len(tile_list) - 1]
        hw_payload = P.n["hwc"]
        act_attn_done = P.n["act"]
        dve_attn_done = dve_tilefree

        P.wait("sw", "hwc", hw_payload)

        def _ccB(h):
            with allow_ici_all_to_all():
                return h.collective_compute(
                    "AllToAll", ALU.bypass,
                    replica_groups=[list(range(NCORE))],
                    ins=[d_ain[-1].ap()], outs=[d_aout[-1].ap()])
        P.q["sw"].append(("raw", _ccB, "cc"))
        P.n["cc"] += 1

        pe_attn_done = P.n["pe"]
        for nm in ("qt0", "kt0", "qt1", "kt1", "vsb", "pbuf",
                   "ptmp", "sq0", "sq1", "hobuf", "recf", "recb", "maskt",
                   "wqb", "wkb", "wvb"):
            ar.free(nm)
        FW = LNW
        NSP = 1
        ar.alloc("rr", 8 * TOKC * 4)
        ar.alloc("tmp", LNW * 4)
        ar.alloc("xres", 4 * LNW * 4)
        ar.alloc("rbf", 2 * LNW * 2)
        ar.alloc("sqb", 2 * LNW * 2)
        ar.alloc("rowa", LNW * 4)
        ar.alloc("rowb", LNW * 4)
        ar.alloc("rowc", LNW * 4)
        ar.alloc("rowbf", 2 * LNW * 2)
        ar.alloc("mbc", 2 * LNW * 2)
        ar.alloc("rbc", 2 * LNW * 2)
        usedB = ar.used()

        v_gsb = V("gsb", BF16, 8 * 2048).rearrange("p (i t) -> p i t", i=8)
        v_rr = V("rr", F32, 8 * TOKC).rearrange("p (i t) -> p i t", i=8)
        v_tmp = V("tmp", F32, LNW)
        v_xres = V("xres", F32, 4 * LNW).rearrange("p (s t) -> p s t", s=4)
        v_grec = V("grecip", BF16, 8 * LNW, parts=2).rearrange("p (i t) -> p i t", i=8)
        v_rbf = V("rbf", BF16, 2 * LNW).rearrange("p (s t) -> p s t", s=2)
        v_sqb = V("sqb", BF16, 2 * LNW).rearrange("p (s t) -> p s t", s=2)
        v_rowa = V("rowa", F32, LNW, parts=1)
        v_rowb = V("rowb", F32, LNW, parts=1)
        v_rowc = V("rowc", F32, LNW, parts=1)
        v_rowbf = V("rowbf", BF16, 2 * LNW, parts=1)
        v_mbc = V("mbc", BF16, 2 * LNW)
        v_rbc = V("rbc", BF16, 2 * LNW)

        aout = [x.ap() for x in d_aout]
        xres_r = i_xres.rearrange("(j p) t -> p j t", p=128)
        outr = o_out.rearrange("(j p) t -> p j t", p=128)

        ln_slot_ctr = [0]

        def emit_ln_stats(src_of_j, off, width, pre_dve, prev_stats_pe):
            nu = width // 512
            slot = ln_slot_ctr[0] % 2
            ln_slot_ctr[0] += 1
            stats_mm = {}
            for j in range(8):
                gate_pe = stats_mm[j - 2] if j >= 2 else prev_stats_pe
                P.wait("dve", "pe", gate_pe)
                P.wait("dve", "dve", pre_dve)
                P.op("dve", "tensor_copy", v_rbf[:, j % 2, 0:width],
                     src_of_j(j)[:, off:off + width])
                rbf_d = P.n["dve"]
                P.wait("act", "pe", gate_pe)
                P.wait("act", "dve", pre_dve)
                P.op("act", "activation", v_sqb[:, j % 2, 0:width],
                     src_of_j(j)[:, off:off + width], AF.Square)
                sq_d = P.n["act"]
                P.wait("pe", "dve", rbf_d)
                P.wait("pe", "act", sq_d)
                for u in range(nu):
                    P.op("pe", "matmul", psB[0:1, 512 * u:512 * u + 512],
                         v_ones, v_rbf[:, j % 2, 512 * u:512 * u + 512],
                         start=(j == 0), stop=(j == 7), skip_group_check=True)
                    P.op("pe", "matmul", psA[0:1, 512 * u:512 * u + 512],
                         v_ones, v_sqb[:, j % 2, 512 * u:512 * u + 512],
                         start=(j == 0), stop=(j == 7), skip_group_check=True,
                         inc="pe" if u == nu - 1 else None)
                stats_mm[j] = P.n["pe"]
            pe_stats = P.n["pe"]
            P.wait("dve", "pe", pe_stats)
            P.op("dve", "tensor_scalar_mul", v_rowa[:, 0:width],
                 psB[0:1, 0:width], 1.0 / D)
            P.edge("dve")
            P.op("dve", "tensor_tensor", v_rowb[:, 0:width], v_rowa[:, 0:width],
                 v_rowa[:, 0:width], ALU.mult)
            P.edge("dve")
            P.op("dve", "scalar_tensor_tensor", v_rowc[:, 0:width],
                 psA[0:1, 0:width], 1.0 / D, v_rowb[:, 0:width],
                 ALU.mult, ALU.subtract)
            dvar = P.n["dve"]
            P.wait("act", "dve", dvar)
            P.op("act", "activation", v_rowb[:, 0:width], v_rowc[:, 0:width],
                 AF.Ln, bias=v_eps[:])
            P.edge("act")
            mrow = v_rowbf[:, 0:width]
            rrow = v_rowbf[:, LNW:LNW + width]
            P.op("act", "activation", rrow, v_rowb[:, 0:width],
                 AF.Exp, scale=-0.5)
            P.op("act", "activation", mrow, v_rowa[:, 0:width], AF.Copy)
            a_rows = P.n["act"]
            P.wait("pe", "act", a_rows)
            for u in range(width // 512):
                P.op("pe", "matmul", psA[:, 512 * u:512 * u + 512], v_onesr,
                     mrow[:, 512 * u:512 * u + 512], start=True, stop=True)
                P.op("pe", "matmul", psB[:, 512 * u:512 * u + 512], v_onesr,
                     rrow[:, 512 * u:512 * u + 512], start=True, stop=True,
                     inc="pe" if u == width // 512 - 1 else None)
            pe_bcast = P.n["pe"]
            P.wait("dve", "pe", pe_bcast)
            P.op("dve", "tensor_copy", v_mbc[:, LNW * slot:LNW * slot + width],
                 psA[:, 0:width])
            P.op("dve", "tensor_copy", v_rbc[:, LNW * slot:LNW * slot + width],
                 psB[:, 0:width])
            return P.n["dve"], slot

        def norm_chunk(src_of_j, off, width, slot, j, dst_bf_of_j=None,
                       out_dma_of_j=None):
            P.edge("dve")
            P.op("dve", "tensor_tensor", v_tmp[0:128, 0:width],
                 src_of_j(j)[:, off:off + width],
                 v_mbc[:, LNW * slot:LNW * slot + width], ALU.subtract)
            P.edge("dve")
            P.op("dve", "tensor_tensor", src_of_j(j)[:, off:off + width],
                 v_tmp[0:128, 0:width],
                 v_rbc[:, LNW * slot:LNW * slot + width], ALU.mult)
            nrm_j = P.n["dve"]
            if dst_bf_of_j is not None:
                P.wait("act", "dve", nrm_j)
                P.op("act", "activation",
                     dst_bf_of_j(j)[:, off:off + width],
                     src_of_j(j)[:, off:off + width], AF.Copy)
            if out_dma_of_j is not None:
                P.wait("hw", "dve", nrm_j)
                P.op("hw", "dma_start", inc="hw",
                     out=out_dma_of_j(j), in_=src_of_j(j)[:, off:off + width])

        def emit_ln_norm(src_of_j, off, width, slot, dst_bf_of_j=None,
                         out_dma_of_j=None):
            for j in range(8):
                norm_chunk(src_of_j, off, width, slot, j, dst_bf_of_j,
                           out_dma_of_j)
            if dst_bf_of_j is not None:
                P.wait("dve", "act", P.n["act"])
                P.op("dve", "engine_nop")
            return P.n["dve"]

        def emit_ln(src_of_j, off, width, pre_dve, prev_stats_pe,
                    dst_bf_of_j=None, out_dma_of_j=None):
            bc_done, slot = emit_ln_stats(src_of_j, off, width, pre_dve,
                                          prev_stats_pe)
            done = emit_ln_norm(src_of_j, off, width, slot, dst_bf_of_j,
                                out_dma_of_j)
            return done, bc_done

        ar.alloc("silb", 2 * FW * 2)
        ar.alloc("ffnh0", FW * 2)
        ar.alloc("ffnh1", FW * 2)
        ar.alloc("wgu0", 8192)
        ar.free("grecip")
        ar.alloc("wgu1", 8192)
        ar.free("wo")
        ar.free("xres")
        for j_ in range(2, 24):
            ar.alloc(f"ffnh{j_}", FW * 2)
        usedC = ar.used()
        print(f"[kernel] arena used: A={usedA} B={usedB} C={usedC} / {AB}")
        ffnh = [V(f"ffnh{j}", BF16, FW) for j in range(24)]
        wgu_off = [ar.blocks["wgu0"][0], ar.blocks["wgu1"][0]]
        v_wgp = [arena[:, wgu_off[p]:wgu_off[p] + 4096]
                 .bitcast(BF16).rearrange("p (d c) -> p d c", d=8)
                 for p in range(2)]
        v_wup = [arena[:, wgu_off[p] + 4096:wgu_off[p] + 8192]
                 .bitcast(BF16).rearrange("p (d c) -> p d c", d=8)
                 for p in range(2)]
        v_wd = [arena[:, wgu_off[p]:wgu_off[p] + 6144]
                .bitcast(BF16).rearrange("p (h c) -> p h c", h=24)
                for p in range(2)]
        wgT_r = i_wgT.rearrange("(d p) h -> p d h", p=128)
        wuT_r = i_wuT.rearrange("(d p) h -> p d h", p=128)
        wdT_r = i_wdT.rearrange("(h p) c -> p h c", p=128)
        v_silb = V("silb", BF16, 2 * FW).rearrange("p (s x) -> p s x", s=2)
        fnu = FW // 512
        wg_rdy_pair = {}
        dve_ffnh = {}
        pe_gu_of = {}

        def load_pair(jj, wait_pe=None, wait_dve=None):
            ps_ = jj % 2
            if wait_pe:
                P.wait("sw", "pe", wait_pe)
            if wait_dve:
                P.wait("sw", "dve", wait_dve)
            P.op("sw", "dma_start", inc="sw", out=v_wgp[ps_],
                 in_=wgT_r[:, :, 256 * jj:256 * jj + 256])
            P.op("sw", "dma_start", inc="sw", out=v_wup[ps_],
                 in_=wuT_r[:, :, 256 * jj:256 * jj + 256])
            wg_rdy_pair[jj] = P.ckpt("sw")

        def emit_gate_up(j, fo, j0_dve=0, act_rows=None, first=False):
            sl = j % 2
            ps_ = (j // 2) % 2
            bank = psA if sl == 0 else psB
            P.wait("pe", "swc", wg_rdy_pair[j // 2])
            if j == 0 or first:
                P.wait("pe", "dve", j0_dve)
            if j >= 2:
                P.wait("pe", "dve", dve_ffnh[j - 2])
            wgv = v_wgp[ps_][:, :, 128 * sl:128 * sl + 128]
            wuv = v_wup[ps_][:, :, 128 * sl:128 * sl + 128]
            for d in range(8):
                if j == 0 and act_rows is not None:
                    P.wait("pe", "act", act_rows[d])
                for u in range(fnu):
                    P.op("pe", "matmul", bank[:, 512 * u:512 * u + 512],
                         wgv[:, d, :],
                         v_gsb[:, d, fo + 512 * u:fo + 512 * u + 512],
                         start=(d == 0), stop=(d == 7),
                         skip_group_check=True)
            for d in range(8):
                for u in range(fnu):
                    P.op("pe", "matmul",
                         bank[:, 1024 + 512 * u:1024 + 512 * u + 512],
                         wuv[:, d, :],
                         v_gsb[:, d, fo + 512 * u:fo + 512 * u + 512],
                         start=(d == 0), stop=(d == 7),
                         skip_group_check=True,
                         inc="pe" if (d == 7 and u == fnu - 1) else None)
            pe_gu = P.n["pe"]
            pe_gu_of[j] = pe_gu
            P.wait("act", "pe", pe_gu)
            P.op("act", "activation", v_silb[:, sl, 0:FW],
                 bank[:, 0:FW], AF.Silu, scale=DEQ)
            a_sil = P.n["act"]
            P.wait("dve", "act", a_sil)
            P.op("dve", "scalar_tensor_tensor", ffnh[j][:, 0:FW],
                 bank[:, 1024:1024 + FW], DEQ, v_silb[:, sl, 0:FW],
                 ALU.mult, ALU.mult)
            dve_ffnh[j] = P.n["dve"]

        if NCHUNK == 1 or NPASS == 1:
            pass
        if not wo_loaded[0]:
            P.op("sw", "dma_start", inc="sw", out=v_wo,
                 in_=i_wo.rearrange("(i p) c -> p i c", p=128))
            wo_loaded[0] = True
        wo_ck = P.ckpt("sw")
        P.wait("hw", "hw", P.n["hw"])
        P.wait("hw", "dve", dve_attn_done)
        P.wait("pe", "act", act_attn_done)
        P.wait("pe", "dve", dve_attn_done)
        prev_win_dve = dve_attn_done
        prev_scale_done = 0
        prev_evac_done = 0
        bank_free = 0
        ln1_state = {}
        prev_gu_pe = 0
        hoisted = False
        ln1_bc0 = 0
        deferred_norm = []
        pe_dn_all = 0
        for hp in range(NPASS):
            lo = LNW * hp
            c0 = lo // HSH
            c1 = (lo + LNW - 1) // HSH
            P.wait("sw", "cc", c1 + 1)
            dve_scale = {}
            if hp in early_grec:
                grec_rdy = early_grec[hp]
            else:
                if hp > 0:
                    P.wait("sw", "dve", prev_scale_done)
                for x in range(c0, c1 + 1):
                    xl = max(lo, HSH * x)
                    xr = min(lo + LNW, HSH * (x + 1))
                    P.op("sw", "dma_start", inc="sw",
                         out=v_grec[:, :, xl - lo:xr - lo],
                         in_=aout[x].rearrange("(i q) t -> q i t", q=130)
                         [128:130, :, xl - HSH * x:xr - HSH * x])
                grec_rdy = P.ckpt("sw")
            if hp in early_gsb:
                gsb_ck = {i: early_gsb[hp][1] for i in range(8)}
            else:
                gsb_ck = {}
                for i in range(8):
                    for x in range(c0, c1 + 1):
                        xl = max(lo, HSH * x)
                        xr = min(lo + LNW, HSH * (x + 1))
                        P.op("sw", "dma_start", inc="sw",
                             out=v_gsb[:, i, xl:xr],
                             in_=aout[x][130 * i:130 * i + 128, xl - HSH * x:
                                         xr - HSH * x])
                    gsb_ck[i] = P.ckpt("sw")
            for i in range(8):
                bank = psA if i % 2 == 0 else psB
                P.wait("pe", "swc", grec_rdy)
                if i >= 2:
                    P.wait("pe", "dve", dve_scale[i - 2])
                elif hp > 0:
                    P.wait("pe", "dve", bank_free)
                for u in range(NU):
                    P.op("pe", "matmul", bank[:, 512 * u:512 * u + 512], v_sel2,
                         v_grec[:, i, 512 * u:512 * u + 512],
                         start=True, stop=True,
                         inc="pe" if u == NU - 1 else None)
                pe_bc = P.n["pe"]
                P.wait("dve", "pe", pe_bc)
                P.wait("dve", "swc", gsb_ck[i])
                P.op("dve", "tensor_tensor", v_gsb[:, i, lo:lo + LNW],
                     v_gsb[:, i, lo:lo + LNW], bank[:, 0:LNW], ALU.mult)
                dve_scale[i] = P.n["dve"]
            prev_scale_done = dve_scale[7]
            pe_scale_all = pe_bc

            if hp > 0:
                P.wait("hw", "dve", prev_evac_done)
            for jj in range(4):
                P.op("hw", "dma_start", inc="hw", out=v_xres[:, jj, 0:LNW],
                     in_=xres_r[:, jj, lo:lo + LNW])
            xres_all = P.ckpt("hw")
            dve_evac = {}
            xres_ck_of = {jj: xres_all for jj in range(4)}
            for jj in range(8):
                if jj >= 4:
                    P.wait("hw", "dve", dve_evac[jj - 4])
                    P.op("hw", "dma_start", inc="hw",
                         out=v_xres[:, jj % 4, 0:LNW],
                         in_=xres_r[:, jj, lo:lo + LNW])
                    xres_ck_of[jj] = P.ckpt("hw")
                bank = psA if jj % 2 == 0 else psB
                if jj == 0:
                    P.wait("pe", "swc", wo_ck)
                P.wait("pe", "dve", dve_scale[7])
                if jj >= 2:
                    P.wait("pe", "dve", dve_evac[jj - 2])
                for i in range(8):
                    for u in range(NU):
                        P.op("pe", "matmul", bank[:, 512 * u:512 * u + 512],
                             v_wo[:, i, 128 * jj:128 * jj + 128],
                             v_gsb[:, i, lo + 512 * u:lo + 512 * u + 512],
                             start=(i == 0), stop=(i == 7),
                             skip_group_check=True,
                             inc="pe" if (i == 7 and u == NU - 1) else None)
                pe_oj = P.n["pe"]
                P.wait("dve", "pe", pe_oj)
                P.wait("dve", "hwc", xres_ck_of[jj])
                P.op("dve", "scalar_tensor_tensor", v_rr[:, jj, lo:lo + LNW],
                     bank[:, 0:LNW], v_bo[:, jj:jj + 1],
                     v_xres[:, jj % 4, 0:LNW], ALU.add, ALU.add)
                dve_evac[jj] = P.n["dve"]
                if deferred_norm:
                    deferred_norm.pop(0)()
            while deferred_norm:
                deferred_norm.pop(0)()
            prev_evac_done = dve_evac[7]
            pe_outproj = P.n["pe"]
            r_done = P.n["dve"]

            bc, slot = emit_ln_stats(lambda j: v_rr[:, j, :], lo, LNW,
                                     pre_dve=r_done,
                                     prev_stats_pe=pe_outproj)
            if hp == 0:
                ln1_rows = {}
                for q_ in range(8):
                    norm_chunk(lambda z: v_rr[:, z, :], lo, LNW, slot, q_,
                               dst_bf_of_j=lambda z: v_gsb[:, z, :])
                    ln1_rows[q_] = P.n["act"]
                ln1_bc0 = bc
                ln1_state[hp] = ("done",)
                if NPASS == 2:
                    load_pair(0, wait_pe=pe_scale_all)
                    emit_gate_up(0, lo, j0_dve=bc, act_rows=ln1_rows)
                    emit_gate_up(1, lo)
                    hoisted = True
                    bank_free = dve_ffnh[1]
                else:
                    bank_free = bc
            else:
                ln1_state[hp] = ("stats", slot, lo)
                bank_free = bc
            prev_win_dve = bc
        pe_outproj_all = P.n["pe"]

        if ln1_state.get(0, (None,))[0] == "done" and not hoisted:
            prev_sp_dve = max(prev_win_dve, ln1_bc0)
        else:
            prev_sp_dve = prev_win_dve
        ffn_hoisted = False
        for hp in range(NPASS):
            if True:
                fo = LNW * hp
                if hp + 1 in ln1_state and ln1_state[hp + 1][0] == "stats":
                    _, nslot, nlo = ln1_state[hp + 1]
                    for q_ in range(8):
                        deferred_norm.append(
                            (lambda q=q_, nslot=nslot, nlo=nlo: norm_chunk(
                                lambda z: v_rr[:, z, :], nlo, LNW, nslot, q,
                                dst_bf_of_j=lambda z: v_gsb[:, z, :])))
                    ln1_state[hp + 1] = ("pending",)

                if hp == 0 and hoisted:
                    j_start = 2
                    load_pair(1, wait_pe=pe_scale_all)
                    load_pair(2, wait_dve=dve_ffnh[1])
                elif hp > 0:
                    j_start = 0
                    load_pair(1, wait_pe=prev_gu_pe)
                else:
                    j_start = 0
                    load_pair(0, wait_pe=pe_scale_all)
                    load_pair(1)
                for j in range(j_start, 24):
                    if deferred_norm and j >= 4:
                        deferred_norm.pop(0)()
                    emit_gate_up(j, fo, j0_dve=prev_sp_dve,
                                 act_rows=(ln1_rows if hp == 0 and
                                           not hoisted and j == 0 else None),
                                 first=(j == j_start))
                    if j % 2 == 1 and j // 2 + 2 < 12:
                        load_pair(j // 2 + 2, wait_dve=dve_ffnh[j])
                while deferred_norm:
                    deferred_norm.pop(0)()
                act_norm_cover = P.n["act"]
                ffnh_done = P.n["dve"]
                prev_gu_pe = P.n["pe"]
                ovl = True
                ln2s_mm = {}
                ln2_rbf = {}

                def emit_ln2_stats_j(q):
                    P.wait("pe", "dve", ln2_rbf[q][0])
                    P.wait("pe", "act", ln2_rbf[q][1])
                    for u in range(fnu):
                        P.op("pe", "matmul",
                             psB[0:1, 1024 + 512 * u:1024 + 512 * u + 512],
                             v_ones, v_rbf[:, q % 2, 512 * u:512 * u + 512],
                             start=(q == 0), stop=(q == 7),
                             skip_group_check=True)
                        P.op("pe", "matmul",
                             psA[0:1, 1024 + 512 * u:1024 + 512 * u + 512],
                             v_ones, v_sqb[:, q % 2, 512 * u:512 * u + 512],
                             start=(q == 0), stop=(q == 7),
                             skip_group_check=True,
                             inc="pe" if u == fnu - 1 else None)
                    ln2s_mm[q] = P.n["pe"]
                dve_r2 = {}
                pe_dn_of = {}
                for j in range(8):
                    sl = j % 2
                    if j >= 2:
                        P.wait("sw", "dve", dve_r2[j - 2])
                    elif j == 0:
                        P.wait("sw", "pe", pe_gu_of[21])
                    else:
                        P.wait("sw", "pe", prev_gu_pe)
                    P.op("sw", "dma_start", inc="sw", out=v_wd[sl],
                         in_=wdT_r[:, :, 128 * j:128 * j + 128])
                    wd_rdy = P.ckpt("sw")
                    if j == 7 and hp + 1 < NPASS:
                        load_pair(0, wait_pe=pe_dn_of[6])
                    bank = psA if sl == 0 else psB
                    P.wait("pe", "swc", wd_rdy)
                    P.wait("pe", "dve", ffnh_done)
                    if j >= 2:
                        P.wait("pe", "dve", dve_r2[j - 2])
                    for hch in range(24):
                        for u in range(fnu):
                            P.op("pe", "matmul", bank[:, 512 * u:512 * u + 512],
                                 v_wd[sl][:, hch, :],
                                 ffnh[hch][:, 512 * u:512 * u + 512],
                                 start=(hch == 0), stop=(hch == 23),
                                 skip_group_check=True,
                                 inc="pe" if (hch == 23 and u == fnu - 1)
                                 else None)
                    if ovl:
                        emit_ln2_stats_j(j - 2) if j >= 2 else None
                    pe_dn = P.n["pe"]
                    pe_dn_of[j] = pe_dn
                    P.wait("dve", "pe", pe_dn)
                    P.edge("dve")
                    P.op("dve", "scalar_tensor_tensor", v_tmp[0:128, 0:FW],
                         bank[:, 0:FW], DEQ, v_rr[:, j, fo:fo + FW],
                         ALU.mult, ALU.add)
                    P.edge("dve")
                    P.op("dve", "tensor_copy", v_rr[:, j, fo:fo + FW],
                         v_tmp[0:128, 0:FW])
                    dve_r2[j] = P.n["dve"]
                    if ovl:
                        if j >= 2:
                            P.wait("dve", "pe", ln2s_mm[j - 2])
                        P.edge("dve")
                        P.op("dve", "tensor_copy", v_rbf[:, j % 2, 0:FW],
                             v_rr[:, j, fo:fo + FW])
                        P.wait("act", "dve", dve_r2[j])
                        if j >= 2:
                            P.wait("act", "pe", ln2s_mm[j - 2])
                        P.op("act", "activation", v_sqb[:, j % 2, 0:FW],
                             v_rr[:, j, fo:fo + FW], AF.Square)
                        ln2_rbf[j] = (P.n["dve"], P.n["act"])
                r2_done = P.n["dve"]
                pe_dn_all = P.n["pe"]
                ffn_hoisted = False
                ln2_slot = ln_slot_ctr[0] % 2
                if ovl:
                    ln_slot_ctr[0] += 1
                    emit_ln2_stats_j(6)
                    emit_ln2_stats_j(7)
                    pe_stats = ln2s_mm[7]
                    P.wait("dve", "pe", pe_stats)
                    P.op("dve", "tensor_scalar_mul", v_rowa[:, 0:FW],
                         psB[0:1, 1024:1024 + FW], 1.0 / D)
                    P.edge("dve")
                    P.op("dve", "tensor_tensor", v_rowb[:, 0:FW],
                         v_rowa[:, 0:FW], v_rowa[:, 0:FW], ALU.mult)
                    P.edge("dve")
                    P.op("dve", "scalar_tensor_tensor", v_rowc[:, 0:FW],
                         psA[0:1, 1024:1024 + FW], 1.0 / D, v_rowb[:, 0:FW],
                         ALU.mult, ALU.subtract)
                    dvar = P.n["dve"]
                    P.wait("act", "dve", dvar)
                    P.op("act", "activation", v_rowb[:, 0:FW],
                         v_rowc[:, 0:FW], AF.Ln, bias=v_eps[:])
                    P.edge("act")
                    mrow = v_rowbf[:, 0:FW]
                    rrow = v_rowbf[:, LNW:LNW + FW]
                    P.op("act", "activation", rrow, v_rowb[:, 0:FW],
                         AF.Exp, scale=-0.5)
                    P.op("act", "activation", mrow, v_rowa[:, 0:FW], AF.Copy)
                    a_rows = P.n["act"]
                    P.wait("pe", "act", a_rows)
                    for u in range(FW // 512):
                        P.op("pe", "matmul", psA[:, 512 * u:512 * u + 512],
                             v_onesr, mrow[:, 512 * u:512 * u + 512],
                             start=True, stop=True)
                        P.op("pe", "matmul", psB[:, 512 * u:512 * u + 512],
                             v_onesr, rrow[:, 512 * u:512 * u + 512],
                             start=True, stop=True,
                             inc="pe" if u == FW // 512 - 1 else None)
                    pe_bcast = P.n["pe"]
                    P.wait("dve", "pe", pe_bcast)
                    P.op("dve", "tensor_copy",
                         v_mbc[:, LNW * ln2_slot:LNW * ln2_slot + FW],
                         psA[:, 0:FW])
                    P.op("dve", "tensor_copy",
                         v_rbc[:, LNW * ln2_slot:LNW * ln2_slot + FW],
                         psB[:, 0:FW])
                    ln2_bc = P.n["dve"]
                else:
                    ln2_bc, ln2_slot = emit_ln_stats(
                        lambda j: v_rr[:, j, :], fo, FW,
                        pre_dve=r2_done, prev_stats_pe=pe_dn_all)
                if hp + 1 < NPASS:
                    for q_ in range(8):
                        deferred_norm.append(
                            (lambda q=q_, sl_=ln2_slot, fo_=fo: norm_chunk(
                                lambda z: v_rr[:, z, :], fo_, FW, sl_, q,
                                out_dma_of_j=lambda z:
                                outr[:, z, fo_:fo_ + FW])))
                else:
                    emit_ln_norm(lambda j: v_rr[:, j, :], fo, FW, ln2_slot,
                                 out_dma_of_j=lambda j: outr[:, j, fo:fo + FW])
                prev_sp_dve = ln2_bc
                prev_gu_pe = pe_dn_all

        out_ckpt = P.ckpt("hw")
        P.wait("sw", "hwc", out_ckpt)
        P.wait("sw", "sw", P.n["sw"])

        if not dense_sync:
            P.finalize_device()
        nops = {k: len(P.q[k]) for k in ENGS}
        nincs = {k: sum(1 for it in P.q[k] if it[0] == "o" and it[4])
                 for k in ENGS}
        print(f"[kernel] plan ops: {nops}; incs: {nincs}; counters {P.n}")

        def runner(name):
            def run(h):
                for item in P.q[name]:
                    if item[0] == "w":
                        _, sname, val = item
                        h.wait_ge(sem[sname], val)
                    elif item[0] == "o":
                        _, method, args, kw, inc = item
                        ins = getattr(h, method)(*args, **kw)
                        if inc is not None:
                            ins.then_inc(sem[inc],
                                         16 if inc in ("hw", "sw") else 1)
                    else:
                        _, fn, inc = item
                        ins = fn(h)
                        ins.then_inc(sem[inc])
            return run

        block = e(nc.Block())
        block.tensor(runner("pe"))
        block.scalar(runner("act"))
        block.vector(runner("dve"))
        block.sync(runner("hw"))
        block.gpsimd(runner("sw"))

    return nc


def stage_inputs(inputs, T):
    x = np.asarray(inputs["L0_attn_x"], np.float32)
    wq = np.asarray(inputs["L0_attn_wq"], np.float32)
    bq = np.asarray(inputs["L0_attn_bq"], np.float32)
    wk = np.asarray(inputs["L0_attn_wk"], np.float32)
    bk = np.asarray(inputs["L0_attn_bk"], np.float32)
    wv = np.asarray(inputs["L0_attn_wv"], np.float32)
    bv = np.asarray(inputs["L0_attn_bv"], np.float32)
    wo = np.ascontiguousarray(np.asarray(inputs["L0_attn_wo"], np.float32))
    bo = np.asarray(inputs["L0_attn_bo"], np.float32)
    wg = np.asarray(inputs["L0_ffn_w_gate"], np.int32)
    wu = np.asarray(inputs["L0_ffn_w_up"], np.int32)
    wd = np.asarray(inputs["L0_ffn_w_down"], np.int32)

    TOKC = 2 * T // NCORE
    xT = [np.ascontiguousarray(x[b].T) for b in range(2)]
    xS = [np.ascontiguousarray(
        xT[b].reshape(8, 128, T // 1024, 1024)
        .transpose(1, 2, 0, 3).reshape(128, -1)) for b in range(2)]
    wgT = np.ascontiguousarray(wg.T)
    wuT = np.ascontiguousarray(wu.T)
    wdT = np.ascontiguousarray(wd.T)
    bo_c = np.ascontiguousarray(bo.reshape(8, 128).T)

    def qkv_s(w, hsl):
        return np.ascontiguousarray(
            w[:, hsl].reshape(8, 128, 128).transpose(1, 0, 2).reshape(128, -1))

    masks = np.zeros((512, 1024), ml_dtypes.bfloat16)
    for j in range(4):
        k = np.arange(128)[:, None]
        q = np.arange(512)[None, :]
        m = (q >= 128 * j + k).astype(ml_dtypes.bfloat16)
        masks[128 * j:128 * j + 128, 0:512] = m
        masks[128 * j:128 * j + 128, 512:1024] = m
    ones = np.ones((128, 1), np.float32)
    onesr = np.ones((1, 128), np.float32)
    sel2 = np.zeros((2, 128), np.float32)
    sel2[0, 0:64] = 1.0
    sel2[1, 64:128] = 1.0

    in_maps = []
    for c in range(NCORE):
        b = (c * TOKC) // T
        t0 = (c * TOKC) % T
        hsl = slice(128 * c, 128 * c + 128)
        in_maps.append({
            "xT0": xS[0], "xT1": xS[1],
            "xres": np.ascontiguousarray(xT[b][:, t0:t0 + TOKC]),
            "wq": qkv_s(wq, hsl),
            "wk": qkv_s(wk, hsl),
            "wv": qkv_s(wv, hsl),
            "bq": np.ascontiguousarray(bq[hsl].reshape(128, 1)),
            "bk": np.ascontiguousarray(bk[hsl].reshape(128, 1)),
            "bvrow": np.ascontiguousarray(bv[hsl].reshape(1, 128)),
            "wo": wo, "bo": bo_c,
            "wgT": wgT, "wuT": wuT, "wdT": wdT,
            "masks": masks, "onescol": ones, "onesrow": onesr, "sel2": sel2,
        })
    return in_maps


def assemble_output(results, T):
    TOKC = 2 * T // NCORE
    out = np.empty((2, T, D), np.float32)
    for c in range(NCORE):
        b = (c * TOKC) // T
        t0 = (c * TOKC) % T
        out[b, t0:t0 + TOKC, :] = np.asarray(results[c]["outT"]).T
    return out


def kernel(**inputs):
    T = int(np.asarray(inputs["L0_attn_x"]).shape[1])
    in_maps = stage_inputs(inputs, T)
    nc = build_nc(T)
    res = run_bass_kernel_spmd(nc, in_maps, core_ids=list(range(NCORE)))
    return assemble_output(res.results, T)


# version 43
# speedup vs baseline: 1.0062x; 1.0060x over previous
import numpy as np
import ml_dtypes
import concourse.bass as bass
import concourse.mybir as mybir
from concourse.bass_utils import run_bass_kernel_spmd
from concourse.aot_env import allow_ici_all_to_all
from contextlib import ExitStack

F32 = mybir.dt.float32
BF16 = mybir.dt.bfloat16
I32 = mybir.dt.int32
U8 = mybir.dt.uint8
AF = mybir.ActivationFunctionType
ALU = mybir.AluOpType
X = mybir.AxisListType.X

D = 1024
HID = 3072
NCORE = 8
SCALE = 0.125
DEQ = 1.0 / 127.0
ENGS = ("pe", "act", "dve", "hw", "sw")


class Arena:
    def __init__(self, size):
        self.size = size
        self.blocks = {}
        self.free_list = [(0, size)]

    def alloc(self, name, nbytes, align=32):
        assert name not in self.blocks
        nbytes = (nbytes + align - 1) // align * align
        for i, (off, sz) in enumerate(self.free_list):
            if nbytes <= sz:
                repl = [(off + nbytes, sz - nbytes)] if sz > nbytes else []
                self.free_list[i:i + 1] = repl
                self.blocks[name] = (off, nbytes)
                return off
        raise MemoryError(f"arena: no room for {name} ({nbytes}B); free={self.free_list}")

    def free(self, name):
        off, sz = self.blocks.pop(name)
        self.free_list.append((off, sz))
        self.free_list.sort()
        out = []
        for b in self.free_list:
            if out and out[-1][0] + out[-1][1] == b[0]:
                out[-1] = (out[-1][0], out[-1][1] + b[1])
            else:
                out.append(b)
        self.free_list = out

    def used(self):
        return self.size - sum(s for _, s in self.free_list)


class Plan:

    def __init__(self):
        self.q = {e: [] for e in ENGS}
        self.n = dict(pe=0, act=0, dve=0, hw=0, sw=0, cc=0, hwc=0, swc=0)

    def wait(self, eng, sname, val):
        if val <= 0:
            return
        if sname in ("hw", "sw"):
            val *= 16
        self.q[eng].append(("w", sname, val))

    def finalize_device(self):
        waited = {}
        for e in ENGS:
            for it in self.q[e]:
                if it[0] == "w":
                    waited.setdefault(it[1], set()).add(it[2])
        remap = {}
        for e in ("dve", "act"):
            w = waited.get(e, set())
            old = new = 0
            m = {0: 0}
            outq = []
            for it in self.q[e]:
                if it[0] == "o" and it[4] == e:
                    old += 1
                    if old in w:
                        new += 1
                        outq.append(it)
                    else:
                        outq.append((it[0], it[1], it[2], it[3], None))
                    m[old] = new
                else:
                    outq.append(it)
            self.q[e] = outq
            remap[e] = m
        for e in ENGS:
            outq = []
            for it in self.q[e]:
                if it[0] == "w" and it[1] in remap:
                    outq.append(("w", it[1], remap[it[1]][it[2]]))
                else:
                    outq.append(it)
            self.q[e] = outq

    def edge(self, eng):
        self.wait(eng, eng, self.n[eng])

    def ckpt(self, q):
        self.wait(q, q, self.n[q])
        self.n[q + "c"] += 1
        self.q[q].append(("o", "nop", (), {}, q + "c"))
        return self.n[q + "c"]

    def op(self, eng, method, *args, inc=None, **kw):
        if eng in ("dve", "act"):
            inc = eng
        if inc is not None:
            self.n[inc] += 1
        self.q[eng].append(("o", method, args, kw, inc))


def build_nc(T, dense_sync=False):
    assert T % 1024 == 0
    TOKG = 2 * T
    TOKC = TOKG // NCORE
    NT = T // 512
    NSC = T // 1024
    NKB = T // 128
    SHARD = TOKC
    assert SHARD % 512 == 0
    NTC = TOKC // 512
    LNW = min(1024, TOKC)
    NPASS = TOKC // LNW
    NU = LNW // 512
    NCH = 2 * NSC

    nc = bass.Bass("TRN2", target_bir_lowering=False)

    i_xT = [nc.dram_tensor(f"xT{b}", [128, 8 * T], F32,
                           kind="ExternalInput").ap() for b in range(2)]
    i_xres = nc.dram_tensor("xres", [D, TOKC], F32, kind="ExternalInput").ap()
    i_wq = nc.dram_tensor("wq", [128, D], F32, kind="ExternalInput").ap()
    i_wk = nc.dram_tensor("wk", [128, D], F32, kind="ExternalInput").ap()
    i_wv = nc.dram_tensor("wv", [128, D], F32, kind="ExternalInput").ap()
    i_bq = nc.dram_tensor("bq", [128, 1], F32, kind="ExternalInput").ap()
    i_bk = nc.dram_tensor("bk", [128, 1], F32, kind="ExternalInput").ap()
    i_bv = nc.dram_tensor("bvrow", [1, 128], F32, kind="ExternalInput").ap()
    i_wo = nc.dram_tensor("wo", [D, D], F32, kind="ExternalInput").ap()
    i_bo = nc.dram_tensor("bo", [128, 8], F32, kind="ExternalInput").ap()
    i_wgT = nc.dram_tensor("wgT", [D, HID], I32, kind="ExternalInput").ap()
    i_wuT = nc.dram_tensor("wuT", [D, HID], I32, kind="ExternalInput").ap()
    i_wdT = nc.dram_tensor("wdT", [HID, D], I32, kind="ExternalInput").ap()
    i_masks = nc.dram_tensor("masks", [512, 1024], BF16, kind="ExternalInput").ap()
    i_ones = nc.dram_tensor("onescol", [128, 1], F32, kind="ExternalInput").ap()
    i_onesr = nc.dram_tensor("onesrow", [1, 128], F32, kind="ExternalInput").ap()
    i_sel2 = nc.dram_tensor("sel2", [2, 128], F32, kind="ExternalInput").ap()
    o_out = nc.dram_tensor("outT", [D, TOKC], F32, kind="ExternalOutput").ap()
    NCHUNK = 2 if SHARD >= 1024 else 1
    HSH = SHARD // NCHUNK
    d_ain = [nc.dram_tensor(f"a2a_in{x}", [8 * 130, HSH], BF16, kind="Internal")
             for x in range(NCHUNK)]
    d_aout = [nc.dram_tensor(f"a2a_out{x}", [8 * 130, HSH], BF16, kind="Internal")
              for x in range(NCHUNK)]

    AB = 207 * 1024
    ar = Arena(AB)
    P = Plan()

    ctx = ExitStack()
    with ctx:
        e = ctx.enter_context
        e(nc.allow_low_precision(
            reason="bf16 matmul operands with f32 accumulation; explicitly "
                   "sanctioned by this problem's stated tolerance"))
        arena = e(nc.sbuf_tensor("arena", [128, AB], U8))
        psA = e(nc.psum_tensor("psA", [128, 2048], F32))
        psB = e(nc.psum_tensor("psB", [128, 2048], F32))
        sem = {k: e(nc.semaphore(f"s_{k}"))
               for k in ("pe", "act", "dve", "hw", "sw", "cc", "hwc", "swc")}

        def V(name, dtype, cols, parts=128, p0=0):
            off, nb = ar.blocks[name]
            esz = mybir.dt.size(dtype)
            assert cols * esz <= nb, (name, cols * esz, nb)
            return arena[p0:p0 + parts, off:off + cols * esz].bitcast(dtype)

        for nm, nb in (("bqkv", 12), ("onescol", 2), ("biasc", 8),
                       ("bvb", 512), ("bvb4", 2048), ("bo", 32), ("sel2", 256), ("onesrow", 256),
                       ("bvrow", 256), ("eps", 4), ("ntmp", 8 * 16 * 4),
                       ("scr", 64), ("wqb", 2048), ("wkb", 2048), ("wvb", 2048)):
            ar.alloc(nm, nb)
        TF = max(T, 8192)
        for b in range(2):
            ar.alloc(f"qt{b}", TF * 2)
            ar.alloc(f"kt{b}", TF * 2)
        NUNIT = 2 * NKB * 2
        ar.alloc("vsb", 2 * (TF // 128) * 2 * 65 * 2)
        ar.alloc("xt0", 16384)
        ar.alloc("xt1", 16384)
        ar.alloc("xt2", 16384)
        ar.alloc("pbuf", 8192)
        ar.alloc("ptmp", 4096)
        ar.alloc("sq0", 1024)
        ar.alloc("sq1", 1024)
        ar.alloc("maskt", 8192)
        ar.alloc("hobuf", 2048)
        ar.alloc("recf", 4096)
        ar.alloc("recb", 4096)
        ar.alloc("wo", 16384)
        usedA = ar.used()

        v_ones = V("onescol", BF16, 1)
        v_onesr = V("onesrow", BF16, 128, parts=1)
        v_sel2 = V("sel2", BF16, 128, parts=2)
        v_bvrow = V("bvrow", BF16, 128, parts=1)
        v_bvb = V("bvb", F32, 128)
        v_bvb4 = V("bvb4", F32, 512).rearrange("p (r c) -> p r c", r=4)
        v_bqkv = V("bqkv", F32, 3)
        v_biasc = V("biasc", F32, 2)
        v_bo = V("bo", F32, 8)
        v_eps = V("eps", F32, 1, parts=1)
        v_ntmp = V("ntmp", F32, 8 * NCH, parts=1)
        v_scr = V("scr", F32, 16, parts=1)
        v_wq = V("wqb", BF16, 1024).rearrange("p (d c) -> p d c", d=8)
        v_wk = V("wkb", BF16, 1024).rearrange("p (d c) -> p d c", d=8)
        v_wv = V("wvb", BF16, 1024).rearrange("p (d c) -> p d c", d=8)
        v_qt = [V(f"qt{b}", BF16, T) for b in range(2)]
        v_kt = [V(f"kt{b}", BF16, T) for b in range(2)]
        v_vsb = V("vsb", BF16, NUNIT * 65).rearrange("p (u c) -> p u c", u=NUNIT)
        v_xt = [V(f"xt{k}", BF16, 8192).rearrange("p (c d t) -> p c d t",
                                                  c=2, d=8) for k in range(3)]
        v_xt_flat = [V("xt0", BF16, 8192), V("xt1", BF16, 8192),
                     V("xt2", BF16, 8192)]
        v_pbuf = V("pbuf", BF16, 4096)
        v_ptmp = V("ptmp", BF16, 2048)
        v_sq = [V("sq0", BF16, 512), V("sq1", BF16, 512)]
        v_maskt = V("maskt", BF16, 4096).rearrange("p (j q) -> p j q", j=4)
        v_hob = V("hobuf", BF16, 1024)
        v_recf = V("recf", F32, 1024, parts=1)
        v_recb = V("recb", BF16, 2048, parts=1)
        v_wo = V("wo", BF16, 8192).rearrange("p (i c) -> p i c", i=8)

        P.op("hw", "dma_start", inc="hw",
             out=V("maskt", BF16, 4096).rearrange("p (j q) -> p j q", j=4),
             in_=i_masks.rearrange("(j p) q -> p j q", p=128))
        P.op("hw", "dma_start", inc="hw", out=v_bqkv[:, 0:1], in_=i_bq)
        P.op("hw", "dma_start", inc="hw", out=v_bqkv[:, 1:2], in_=i_bk)
        P.op("hw", "dma_start", inc="hw", out=v_bo, in_=i_bo)
        hw_consts = P.ckpt("hw")
        sc_list = [(b, sc) for b in range(2) for sc in range(NSC)]
        xt_ready = {}

        def emit_xt_load(idx, wait_pe):
            b, sc = sc_list[idx]
            P.wait("sw", "pe", wait_pe)
            P.op("sw", "dma_start", inc="sw", out=v_xt_flat[idx % 3],
                 in_=i_xT[b][:, 8192 * sc:8192 * sc + 8192])
            ck = P.ckpt("sw")
            xt_ready[idx] = [ck, ck]

        P.op("sw", "dma_start", inc="sw", out=v_xt_flat[0][:, 0:4096],
             in_=i_xT[0][:, 0:4096])
        P.op("sw", "dma_start", inc="sw", out=V("wqb", BF16, 1024), in_=i_wq)
        P.op("sw", "dma_start", inc="sw", out=V("wkb", BF16, 1024), in_=i_wk)
        P.op("sw", "dma_start", inc="sw", out=V("wvb", BF16, 1024), in_=i_wv)
        ck0a = P.ckpt("sw")
        P.op("sw", "dma_start", inc="sw", out=v_ones, in_=i_ones)
        P.op("sw", "dma_start", inc="sw", out=v_onesr, in_=i_onesr)
        P.op("sw", "dma_start", inc="sw", out=v_sel2, in_=i_sel2)
        P.op("sw", "dma_start", inc="sw", out=v_bvrow, in_=i_bv)
        sw_consts = P.ckpt("sw")
        P.op("sw", "dma_start", inc="sw", out=v_xt_flat[0][:, 4096:8192],
             in_=i_xT[0][:, 4096:8192])
        ck0b = P.ckpt("sw")
        xt_ready[0] = [ck0a, ck0b]

        P.wait("dve", "hwc", hw_consts)
        P.wait("act", "hwc", hw_consts)
        P.op("dve", "memset", v_eps[:], 1e-5)
        P.op("dve", "memset", v_ptmp[:], 0.0)
        P.op("dve", "memset", v_vsb[:, :, 64:65], 1.0)
        P.wait("pe", "swc", sw_consts)
        P.op("pe", "matmul", psB[:, 0:128], v_onesr, v_bvrow,
             start=True, stop=True, inc="pe")
        P.wait("dve", "pe", P.n["pe"])
        P.op("dve", "tensor_copy", v_bvb, psB[:, 0:128])
        P.edge("dve")
        for r_ in range(4):
            P.op("dve", "tensor_copy", v_bvb4[:, r_, :], v_bvb)
        dve_bvb = P.n["dve"]

        if len(sc_list) > 1:
            emit_xt_load(1, 0)
        if len(sc_list) > 2:
            emit_xt_load(2, 0)

        wo_loaded = [False]
        prev_qk_act = 0
        prev_v_ev = {0: dve_bvb, 1: dve_bvb}
        prev_norm_ev = 0
        for idx, (b, sc) in enumerate(sc_list):
            slot = idx % 3
            pe_v_last = 0
            for tc2 in range(2):
                P.wait("pe", "swc", xt_ready[idx][tc2])
                tt = 1024 * sc + 512 * tc2
                tl = 512 * tc2
                P.wait("pe", "act", prev_qk_act)
                for d in range(8):
                    P.op("pe", "matmul", psA[:, 0:512], v_wq[:, d, :],
                         v_xt[slot][:, tc2, d, :],
                         start=(d == 0), stop=(d == 7),
                         inc="pe" if d == 7 else None)
                pe_q = P.n["pe"]
                for d in range(8):
                    P.op("pe", "matmul", psA[:, 512:1024], v_wk[:, d, :],
                         v_xt[slot][:, tc2, d, :],
                         start=(d == 0), stop=(d == 7),
                         inc="pe" if d == 7 else None)
                pe_k = P.n["pe"]
                P.wait("act", "pe", pe_q)
                P.op("act", "activation", v_sq[0][:], psA[:, 0:512], AF.Square,
                     bias=v_bqkv[:, 0:1])
                P.op("act", "add", v_qt[b][:, tt:tt + 512],
                     psA[:, 0:512], v_bqkv[:, 0:1])
                act_q = P.n["act"]
                P.wait("act", "pe", pe_k)
                P.op("act", "activation", v_sq[1][:], psA[:, 512:1024], AF.Square,
                     bias=v_bqkv[:, 1:2])
                P.op("act", "add", v_kt[b][:, tt:tt + 512],
                     psA[:, 512:1024], v_bqkv[:, 1:2])
                act_k = P.n["act"]
                prev_qk_act = act_k
                vb = tc2
                P.wait("pe", "dve", prev_v_ev[vb])
                for tb in range(4):
                    for d in range(8):
                        P.op("pe", "matmul",
                             psB[:, 512 * vb + 128 * tb:512 * vb + 128 * tb + 128],
                             v_xt[slot][:, tc2, d, 128 * tb:128 * tb + 128],
                             v_wv[:, d, :],
                             start=(tb == 0 and d == 0),
                             stop=(tb == 3 and d == 7), skip_group_check=True,
                             inc="pe" if (tb == 3 and d == 7) else None)
                pe_v = P.n["pe"]
                pe_v_last = pe_v
                P.wait("pe", "act", act_k)
                P.wait("pe", "dve", prev_norm_ev)
                ndst = [psA[0:1, 1024:1536], psA[0:1, 1536:2048],
                        psB[0:1, 1024:1536], psB[0:1, 1536:2048]]
                for v in range(4):
                    half, sq = v % 2, v // 2
                    P.op("pe", "matmul", ndst[v],
                         v_ones[64 * half:64 * half + 64, :],
                         v_sq[sq][64 * half:64 * half + 64, :],
                         start=True, stop=True,
                         tile_position=(64 * half, 0),
                         inc="pe" if v == 3 else None)
                pe_norm = P.n["pe"]
                P.wait("dve", "pe", pe_norm)
                ch = sc * 2 + tc2
                for v in range(4):
                    dst = (b * 4 + v) * NCH + ch
                    P.op("dve", "reduce_max", v_ntmp[:, dst:dst + 1],
                         ndst[v], axis=X)
                prev_norm_ev = P.n["dve"]
                P.wait("dve", "pe", pe_v)
                kb0 = tt // 128
                u0 = (b * NKB + kb0) * 2
                for hh in range(2):
                    P.op("dve", "tensor_tensor",
                         v_vsb[:, u0 + hh:u0 + hh + 7:2, 0:64],
                         psB[:, 512 * vb:512 * vb + 512].rearrange(
                             "p (tb c) -> p tb c", tb=4)[:, :, 64 * hh:
                                                         64 * hh + 64],
                         v_bvb4[:, :, 64 * hh:64 * hh + 64], ALU.add)
                prev_v_ev[vb] = P.n["dve"]
            if idx + 3 < len(sc_list):
                emit_xt_load(idx + 3, pe_v_last)

        P.edge("dve")
        for b in range(2):
            for v in range(4):
                base = (b * 4 + v) * NCH
                P.op("dve", "tensor_reduce", v_scr[:, v:v + 1],
                     v_ntmp[:, base:base + NCH], X, ALU.max)
            P.edge("dve")
            P.op("dve", "tensor_tensor", v_scr[:, 4:5], v_scr[:, 0:1],
                 v_scr[:, 2:3], ALU.mult)
            P.op("dve", "tensor_tensor", v_scr[:, 5:6], v_scr[:, 1:2],
                 v_scr[:, 3:4], ALU.mult)
            P.edge("dve")
            P.op("dve", "tensor_tensor", v_scr[:, 6:7], v_scr[:, 4:5],
                 v_scr[:, 5:6], ALU.max)
            P.wait("act", "dve", P.n["dve"])
            P.op("act", "activation", v_scr[:, 7:8], v_scr[:, 6:7], AF.Sqrt)
            P.wait("dve", "act", P.n["act"])
            P.op("dve", "tensor_scalar_mul", v_scr[:, 7:8], v_scr[:, 7:8],
                 -(SCALE * 1.0103))
            P.edge("dve")
            vb16 = v_scr[:, 12:13].bitcast(BF16)[:, 0:1]
            P.op("dve", "tensor_copy", vb16, v_scr[:, 7:8])
            P.wait("pe", "dve", P.n["dve"])
            P.op("pe", "matmul", psB[:, 128:129], v_onesr, vb16,
                 start=True, stop=True, inc="pe")
            P.wait("dve", "pe", P.n["pe"])
            P.op("dve", "tensor_copy", v_biasc[:, b:b + 1], psB[:, 128:129])
        dve_bias = P.n["dve"]

        xt0_off = ar.blocks["xt0"][0]
        ar.free("xt0")
        ar.free("xt1")
        ar.free("xt2")
        gsb_off = ar.alloc("gsb", 8 * 4096)
        assert gsb_off == xt0_off, (gsb_off, xt0_off)
        ar.alloc("grecip", 8 * LNW * 2)
        v_gsb_early = V("gsb", BF16, 8 * 2048).rearrange("p (i t) -> p i t", i=8)
        v_grec_early = V("grecip", BF16, 8 * LNW,
                         parts=2).rearrange("p (i t) -> p i t", i=8)
        early_gsb = {}
        early_grec = {}
        P.wait("act", "dve", dve_bias)
        P.wait("pe", "dve", dve_bias)
        g = 0
        act_of_g = {}
        mask_of_g = {}
        ptmp_user = {}
        tile_ep = {}
        hw_tile = {}
        tidx = 0
        ain = [x.ap() for x in d_ain]

        def chunk_of(b, t):
            return ((b * T + 512 * t) % SHARD) // HSH
        tile_list = [(b, t) for c in range(NCHUNK)
                     for b in range(2) for t in range(NT) if chunk_of(b, t) == c]
        n_tiles_A = (sum(1 for b, t in tile_list if chunk_of(b, t) == 0)
                     if NCHUNK == 2 else -1)
        pending_tail = [None]
        for ti, (b, t) in enumerate(tile_list):
            if True:
                nkb = 4 * t + 4
                bko = 1024 * (ti % 2)
                gbase = g

                def emit_av(pkb, gbase=gbase, nkb=nkb, bko=bko, b=b, ti=ti):
                    psl = (gbase + pkb) % 4
                    if pkb == 0:
                        P.wait("pe", "dve", tile_ep.get(ti - 2, 0))
                    gg = gbase + pkb
                    if gg in mask_of_g:
                        P.wait("pe", "dve", mask_of_g[gg])
                    else:
                        P.wait("pe", "act", act_of_g[gg])
                    for hh in range(2):
                        u = (b * NKB + pkb) * 2 + hh
                        P.op("pe", "matmul",
                             psB[0:65, bko + 512 * hh:bko + 512 * hh + 512],
                             v_vsb[:, u, :],
                             v_pbuf[:, 1024 * psl + 512 * hh:
                                    1024 * psl + 512 * hh + 512],
                             start=(pkb == 0), stop=(pkb == nkb - 1),
                             skip_group_check=True,
                             inc="pe" if (hh == 1 and pkb == nkb - 1) else None)

                for kb in range(nkb):
                    sl2 = g % 2
                    sl3 = g % 4
                    diag = kb >= 4 * t
                    j = kb - 4 * t
                    if g >= 2:
                        P.wait("pe", "act", act_of_g[g - 2])
                    P.op("pe", "matmul", psA[:, 1024 * sl2:1024 * sl2 + 512],
                         v_kt[b][0:64, 128 * kb:128 * kb + 128],
                         v_qt[b][0:64, 512 * t:512 * t + 512],
                         start=True, stop=True, tile_position=(0, 0))
                    P.op("pe", "matmul", psA[:, 1024 * sl2 + 512:1024 * sl2 + 1024],
                         v_kt[b][64:128, 128 * kb:128 * kb + 128],
                         v_qt[b][64:128, 512 * t:512 * t + 512],
                         start=True, stop=True, tile_position=(64, 0), inc="pe")
                    pe_s = P.n["pe"]
                    P.wait("act", "pe", pe_s)
                    if diag:
                        pts = g % 2
                        P.wait("act", "dve", ptmp_user.get(pts, 0))
                        if j >= 2:
                            c = 128 * j
                            for hh in range(2):
                                P.op("act", "activation",
                                     v_ptmp[:, 1024 * pts + 512 * hh + c:
                                            1024 * pts + 512 * hh + 512],
                                     psA[:, 1024 * sl2 + 512 * hh + c:
                                         1024 * sl2 + 512 * hh + 512],
                                     AF.Exp, bias=v_biasc[:, b:b + 1],
                                     scale=SCALE)
                        else:
                            P.op("act", "activation",
                                 v_ptmp[:, 1024 * pts:1024 * pts + 1024],
                                 psA[:, 1024 * sl2:1024 * sl2 + 1024], AF.Exp,
                                 bias=v_biasc[:, b:b + 1], scale=SCALE)
                    else:
                        P.op("act", "activation",
                             v_pbuf[:, 1024 * sl3:1024 * sl3 + 1024],
                             psA[:, 1024 * sl2:1024 * sl2 + 1024], AF.Exp,
                             bias=v_biasc[:, b:b + 1], scale=SCALE)
                    act_of_g[g] = P.n["act"]
                    if diag:
                        P.wait("dve", "act", act_of_g[g])
                        P.op("dve", "tensor_tensor",
                             v_pbuf[:, 1024 * sl3:1024 * sl3 + 1024],
                             v_ptmp[:, 1024 * pts:1024 * pts + 1024],
                             v_maskt[:, j, :], ALU.mult)
                        mask_of_g[g] = P.n["dve"]
                        ptmp_user[pts] = P.n["dve"]
                    if kb >= 2:
                        emit_av(kb - 2)
                    g += 1
                    if kb == 1 and pending_tail[0] is not None:
                        pending_tail[0]()
                        pending_tail[0] = None

                def tile_tail(emit_av=emit_av, nkb=nkb, bko=bko, b=b, t=t,
                              ti=ti):
                    emit_av(nkb - 2)
                    emit_av(nkb - 1)
                    pe_av = P.n["pe"]
                    hsl = ti % 2
                    last_tile = (ti == len(tile_list) - 1)
                    P.wait("dve", "pe", pe_av)
                    if ti >= 2:
                        P.wait("dve", "hwc", hw_tile[ti - 2])
                    P.edge("dve")
                    if last_tile:
                        P.wait("act", "pe", pe_av)
                        if ti >= 1:
                            P.wait("act", "dve", tile_ep[ti - 1])
                        if ti >= 2:
                            P.wait("act", "hwc", hw_tile[ti - 2])
                        P.op("act", "activation", v_recf[:, 0:1024],
                             psB[64:65, bko:bko + 1024], AF.Ln)
                        P.edge("act")
                        P.op("act", "activation",
                             v_recb[:, 1024 * hsl:1024 * hsl + 1024],
                             v_recf[:, 0:1024], AF.Exp, scale=-1.0)
                    else:
                        P.op("dve", "reciprocal", v_recf[:, 0:512],
                             psB[64:65, bko:bko + 512])
                        P.edge("dve")
                        P.op("dve", "tensor_copy",
                             v_recb[:, 1024 * hsl:1024 * hsl + 512],
                             v_recf[:, 0:512])
                        P.op("dve", "reciprocal", v_recf[:, 512:1024],
                             psB[64:65, bko + 512:bko + 1024])
                        P.edge("dve")
                        P.op("dve", "tensor_copy",
                             v_recb[:, 1024 * hsl + 512:1024 * hsl + 1024],
                             v_recf[:, 512:1024])
                    P.op("dve", "tensor_copy",
                         v_hob[0:64, 512 * hsl:512 * hsl + 512],
                         psB[0:64, bko:bko + 512])
                    P.op("dve", "tensor_copy",
                         v_hob[64:128, 512 * hsl:512 * hsl + 512],
                         psB[0:64, bko + 512:bko + 1024])
                    tile_ep[ti] = P.n["dve"]
                    gt0 = b * T + 512 * t
                    sh = gt0 // SHARD
                    ch = chunk_of(b, t)
                    c0 = (gt0 % SHARD) % HSH
                    P.wait("hw", "dve", tile_ep[ti])
                    if last_tile:
                        P.wait("hw", "act", P.n["act"])
                    P.op("hw", "dma_start", inc="hw",
                         out=ain[ch][130 * sh:130 * sh + 128, c0:c0 + 512],
                         in_=v_hob[:, 512 * hsl:512 * hsl + 512])
                    P.op("hw", "dma_start", inc="hw",
                         out=ain[ch][130 * sh + 128:130 * sh + 129, c0:c0 + 512],
                         in_=v_recb[:, 1024 * hsl:1024 * hsl + 512])
                    P.op("hw", "dma_start", inc="hw",
                         out=ain[ch][130 * sh + 129:130 * sh + 130, c0:c0 + 512],
                         in_=v_recb[:, 1024 * hsl + 512:1024 * hsl + 1024])
                    hw_tile[ti] = P.ckpt("hw")
                    if ti + 1 == n_tiles_A:
                        P.wait("sw", "hwc", hw_tile[ti])

                        def _ccA(h):
                            with allow_ici_all_to_all():
                                return h.collective_compute(
                                    "AllToAll", ALU.bypass,
                                    replica_groups=[list(range(NCORE))],
                                    ins=[d_ain[0].ap()], outs=[d_aout[0].ap()])
                        P.q["sw"].append(("raw", _ccA, "cc"))
                        P.n["cc"] += 1
                        P.op("sw", "dma_start", inc="sw", out=v_wo,
                             in_=i_wo.rearrange("(i p) c -> p i c", p=128))
                        wo_loaded[0] = True
                        if NCHUNK == 2 and NPASS == 2:
                            P.wait("sw", "cc", 1)
                            for i_ in range(8):
                                P.op("sw", "dma_start", inc="sw",
                                     out=v_gsb_early[:, i_, 0:HSH],
                                     in_=d_aout[0].ap()
                                     [130 * i_:130 * i_ + 128, :])
                            early_gsb[0] = (0, P.ckpt("sw"))
                            P.op("sw", "dma_start", inc="sw",
                                 out=v_grec_early[:, :, 0:HSH],
                                 in_=d_aout[0].ap().rearrange(
                                     "(i q) t -> q i t", q=130)[128:130, :, :])
                            early_grec[0] = P.ckpt("sw")
                pending_tail[0] = tile_tail
        pending_tail[0]()
        pending_tail[0] = None
        dve_tilefree = tile_ep[len(tile_list) - 1]
        hw_payload = P.n["hwc"]
        act_attn_done = P.n["act"]
        dve_attn_done = dve_tilefree

        P.wait("sw", "hwc", hw_payload)

        def _ccB(h):
            with allow_ici_all_to_all():
                return h.collective_compute(
                    "AllToAll", ALU.bypass,
                    replica_groups=[list(range(NCORE))],
                    ins=[d_ain[-1].ap()], outs=[d_aout[-1].ap()])
        P.q["sw"].append(("raw", _ccB, "cc"))
        P.n["cc"] += 1

        pe_attn_done = P.n["pe"]
        for nm in ("qt0", "kt0", "qt1", "kt1", "vsb", "pbuf",
                   "ptmp", "sq0", "sq1", "hobuf", "recf", "recb", "maskt",
                   "wqb", "wkb", "wvb"):
            ar.free(nm)
        FW = LNW
        NSP = 1
        ar.alloc("rr", 8 * TOKC * 4)
        ar.alloc("tmp", LNW * 4)
        ar.alloc("xres", 4 * LNW * 4)
        ar.alloc("rbf", 2 * LNW * 2)
        ar.alloc("sqb", 2 * LNW * 2)
        ar.alloc("rowa", LNW * 4)
        ar.alloc("rowb", LNW * 4)
        ar.alloc("rowc", LNW * 4)
        ar.alloc("rowbf", 2 * LNW * 2)
        ar.alloc("mbc", 2 * LNW * 2)
        ar.alloc("rbc", 2 * LNW * 2)
        usedB = ar.used()

        v_gsb = V("gsb", BF16, 8 * 2048).rearrange("p (i t) -> p i t", i=8)
        v_rr = V("rr", F32, 8 * TOKC).rearrange("p (i t) -> p i t", i=8)
        v_tmp = V("tmp", F32, LNW)
        v_xres = V("xres", F32, 4 * LNW).rearrange("p (s t) -> p s t", s=4)
        v_grec = V("grecip", BF16, 8 * LNW, parts=2).rearrange("p (i t) -> p i t", i=8)
        v_rbf = V("rbf", BF16, 2 * LNW).rearrange("p (s t) -> p s t", s=2)
        v_sqb = V("sqb", BF16, 2 * LNW).rearrange("p (s t) -> p s t", s=2)
        v_rowa = V("rowa", F32, LNW, parts=1)
        v_rowb = V("rowb", F32, LNW, parts=1)
        v_rowc = V("rowc", F32, LNW, parts=1)
        v_rowbf = V("rowbf", BF16, 2 * LNW, parts=1)
        v_mbc = V("mbc", BF16, 2 * LNW)
        v_rbc = V("rbc", BF16, 2 * LNW)

        aout = [x.ap() for x in d_aout]
        xres_r = i_xres.rearrange("(j p) t -> p j t", p=128)
        outr = o_out.rearrange("(j p) t -> p j t", p=128)

        ln_slot_ctr = [0]

        def emit_ln_stats(src_of_j, off, width, pre_dve, prev_stats_pe):
            nu = width // 512
            slot = ln_slot_ctr[0] % 2
            ln_slot_ctr[0] += 1
            stats_mm = {}
            for j in range(8):
                gate_pe = stats_mm[j - 2] if j >= 2 else prev_stats_pe
                P.wait("dve", "pe", gate_pe)
                P.wait("dve", "dve", pre_dve)
                P.op("dve", "tensor_copy", v_rbf[:, j % 2, 0:width],
                     src_of_j(j)[:, off:off + width])
                rbf_d = P.n["dve"]
                P.wait("act", "pe", gate_pe)
                P.wait("act", "dve", pre_dve)
                P.op("act", "activation", v_sqb[:, j % 2, 0:width],
                     src_of_j(j)[:, off:off + width], AF.Square)
                sq_d = P.n["act"]
                P.wait("pe", "dve", rbf_d)
                P.wait("pe", "act", sq_d)
                for u in range(nu):
                    P.op("pe", "matmul", psB[0:1, 512 * u:512 * u + 512],
                         v_ones, v_rbf[:, j % 2, 512 * u:512 * u + 512],
                         start=(j == 0), stop=(j == 7), skip_group_check=True)
                    P.op("pe", "matmul", psA[0:1, 512 * u:512 * u + 512],
                         v_ones, v_sqb[:, j % 2, 512 * u:512 * u + 512],
                         start=(j == 0), stop=(j == 7), skip_group_check=True,
                         inc="pe" if u == nu - 1 else None)
                stats_mm[j] = P.n["pe"]
            pe_stats = P.n["pe"]
            P.wait("dve", "pe", pe_stats)
            P.op("dve", "tensor_scalar_mul", v_rowa[:, 0:width],
                 psB[0:1, 0:width], 1.0 / D)
            P.edge("dve")
            P.op("dve", "tensor_tensor", v_rowb[:, 0:width], v_rowa[:, 0:width],
                 v_rowa[:, 0:width], ALU.mult)
            P.edge("dve")
            P.op("dve", "scalar_tensor_tensor", v_rowc[:, 0:width],
                 psA[0:1, 0:width], 1.0 / D, v_rowb[:, 0:width],
                 ALU.mult, ALU.subtract)
            dvar = P.n["dve"]
            P.wait("act", "dve", dvar)
            P.op("act", "activation", v_rowb[:, 0:width], v_rowc[:, 0:width],
                 AF.Ln, bias=v_eps[:])
            P.edge("act")
            mrow = v_rowbf[:, 0:width]
            rrow = v_rowbf[:, LNW:LNW + width]
            P.op("act", "activation", rrow, v_rowb[:, 0:width],
                 AF.Exp, scale=-0.5)
            P.op("act", "activation", mrow, v_rowa[:, 0:width], AF.Copy)
            a_rows = P.n["act"]
            P.wait("pe", "act", a_rows)
            for u in range(width // 512):
                P.op("pe", "matmul", psA[:, 512 * u:512 * u + 512], v_onesr,
                     mrow[:, 512 * u:512 * u + 512], start=True, stop=True)
                P.op("pe", "matmul", psB[:, 512 * u:512 * u + 512], v_onesr,
                     rrow[:, 512 * u:512 * u + 512], start=True, stop=True,
                     inc="pe" if u == width // 512 - 1 else None)
            pe_bcast = P.n["pe"]
            P.wait("dve", "pe", pe_bcast)
            P.op("dve", "tensor_copy", v_mbc[:, LNW * slot:LNW * slot + width],
                 psA[:, 0:width])
            P.op("dve", "tensor_copy", v_rbc[:, LNW * slot:LNW * slot + width],
                 psB[:, 0:width])
            return P.n["dve"], slot

        def norm_chunk(src_of_j, off, width, slot, j, dst_bf_of_j=None,
                       out_dma_of_j=None):
            P.edge("dve")
            P.op("dve", "tensor_tensor", v_tmp[0:128, 0:width],
                 src_of_j(j)[:, off:off + width],
                 v_mbc[:, LNW * slot:LNW * slot + width], ALU.subtract)
            P.edge("dve")
            P.op("dve", "tensor_tensor", src_of_j(j)[:, off:off + width],
                 v_tmp[0:128, 0:width],
                 v_rbc[:, LNW * slot:LNW * slot + width], ALU.mult)
            nrm_j = P.n["dve"]
            if dst_bf_of_j is not None:
                P.wait("act", "dve", nrm_j)
                P.op("act", "activation",
                     dst_bf_of_j(j)[:, off:off + width],
                     src_of_j(j)[:, off:off + width], AF.Copy)
            if out_dma_of_j is not None:
                P.wait("hw", "dve", nrm_j)
                P.op("hw", "dma_start", inc="hw",
                     out=out_dma_of_j(j), in_=src_of_j(j)[:, off:off + width])

        def emit_ln_norm(src_of_j, off, width, slot, dst_bf_of_j=None,
                         out_dma_of_j=None):
            for j in range(8):
                norm_chunk(src_of_j, off, width, slot, j, dst_bf_of_j,
                           out_dma_of_j)
            if dst_bf_of_j is not None:
                P.wait("dve", "act", P.n["act"])
                P.op("dve", "engine_nop")
            return P.n["dve"]

        def emit_ln(src_of_j, off, width, pre_dve, prev_stats_pe,
                    dst_bf_of_j=None, out_dma_of_j=None):
            bc_done, slot = emit_ln_stats(src_of_j, off, width, pre_dve,
                                          prev_stats_pe)
            done = emit_ln_norm(src_of_j, off, width, slot, dst_bf_of_j,
                                out_dma_of_j)
            return done, bc_done

        ar.alloc("silb", 2 * FW * 2)
        ar.alloc("ffnh0", FW * 2)
        ar.alloc("ffnh1", FW * 2)
        ar.alloc("wgu0", 8192)
        ar.free("grecip")
        ar.alloc("wgu1", 8192)
        ar.free("wo")
        ar.free("xres")
        for j_ in range(2, 24):
            ar.alloc(f"ffnh{j_}", FW * 2)
        usedC = ar.used()
        print(f"[kernel] arena used: A={usedA} B={usedB} C={usedC} / {AB}")
        ffnh = [V(f"ffnh{j}", BF16, FW) for j in range(24)]
        wgu_off = [ar.blocks["wgu0"][0], ar.blocks["wgu1"][0]]
        v_wgp = [arena[:, wgu_off[p]:wgu_off[p] + 4096]
                 .bitcast(BF16).rearrange("p (d c) -> p d c", d=8)
                 for p in range(2)]
        v_wup = [arena[:, wgu_off[p] + 4096:wgu_off[p] + 8192]
                 .bitcast(BF16).rearrange("p (d c) -> p d c", d=8)
                 for p in range(2)]
        v_wd = [arena[:, wgu_off[p]:wgu_off[p] + 6144]
                .bitcast(BF16).rearrange("p (h c) -> p h c", h=24)
                for p in range(2)]
        wgT_r = i_wgT.rearrange("(d p) h -> p d h", p=128)
        wuT_r = i_wuT.rearrange("(d p) h -> p d h", p=128)
        wdT_r = i_wdT.rearrange("(h p) c -> p h c", p=128)
        v_silb = V("silb", BF16, 2 * FW).rearrange("p (s x) -> p s x", s=2)
        fnu = FW // 512
        wg_rdy_pair = {}
        dve_ffnh = {}
        pe_gu_of = {}

        def load_pair(jj, wait_pe=None, wait_dve=None):
            ps_ = jj % 2
            if wait_pe:
                P.wait("sw", "pe", wait_pe)
            if wait_dve:
                P.wait("sw", "dve", wait_dve)
            P.op("sw", "dma_start", inc="sw", out=v_wgp[ps_],
                 in_=wgT_r[:, :, 256 * jj:256 * jj + 256])
            P.op("sw", "dma_start", inc="sw", out=v_wup[ps_],
                 in_=wuT_r[:, :, 256 * jj:256 * jj + 256])
            wg_rdy_pair[jj] = P.ckpt("sw")

        def emit_gate_up(j, fo, j0_dve=0, act_rows=None, first=False):
            sl = j % 2
            ps_ = (j // 2) % 2
            bank = psA if sl == 0 else psB
            P.wait("pe", "swc", wg_rdy_pair[j // 2])
            if j == 0 or first:
                P.wait("pe", "dve", j0_dve)
            if j >= 2:
                P.wait("pe", "dve", dve_ffnh[j - 2])
            wgv = v_wgp[ps_][:, :, 128 * sl:128 * sl + 128]
            wuv = v_wup[ps_][:, :, 128 * sl:128 * sl + 128]
            for d in range(8):
                if j == 0 and act_rows is not None:
                    P.wait("pe", "act", act_rows[d])
                for u in range(fnu):
                    P.op("pe", "matmul", bank[:, 512 * u:512 * u + 512],
                         wgv[:, d, :],
                         v_gsb[:, d, fo + 512 * u:fo + 512 * u + 512],
                         start=(d == 0), stop=(d == 7),
                         skip_group_check=True)
            for d in range(8):
                for u in range(fnu):
                    P.op("pe", "matmul",
                         bank[:, 1024 + 512 * u:1024 + 512 * u + 512],
                         wuv[:, d, :],
                         v_gsb[:, d, fo + 512 * u:fo + 512 * u + 512],
                         start=(d == 0), stop=(d == 7),
                         skip_group_check=True,
                         inc="pe" if (d == 7 and u == fnu - 1) else None)
            pe_gu = P.n["pe"]
            pe_gu_of[j] = pe_gu
            P.wait("act", "pe", pe_gu)
            P.op("act", "activation", v_silb[:, sl, 0:FW],
                 bank[:, 0:FW], AF.Silu, scale=DEQ)
            a_sil = P.n["act"]
            P.wait("dve", "act", a_sil)
            P.op("dve", "scalar_tensor_tensor", ffnh[j][:, 0:FW],
                 bank[:, 1024:1024 + FW], DEQ, v_silb[:, sl, 0:FW],
                 ALU.mult, ALU.mult)
            dve_ffnh[j] = P.n["dve"]

        if NCHUNK == 1 or NPASS == 1:
            pass
        if not wo_loaded[0]:
            P.op("sw", "dma_start", inc="sw", out=v_wo,
                 in_=i_wo.rearrange("(i p) c -> p i c", p=128))
            wo_loaded[0] = True
        wo_ck = P.ckpt("sw")
        P.wait("hw", "hw", P.n["hw"])
        P.wait("hw", "dve", dve_attn_done)
        P.wait("pe", "act", act_attn_done)
        P.wait("pe", "dve", dve_attn_done)
        prev_win_dve = dve_attn_done
        prev_scale_done = 0
        prev_evac_done = 0
        bank_free = 0
        ln1_state = {}
        prev_gu_pe = 0
        hoisted = False
        ln1_bc0 = 0
        deferred_norm = []
        pe_dn_all = 0
        for hp in range(NPASS):
            lo = LNW * hp
            c0 = lo // HSH
            c1 = (lo + LNW - 1) // HSH
            P.wait("sw", "cc", c1 + 1)
            dve_scale = {}
            if hp in early_grec:
                grec_rdy = early_grec[hp]
            else:
                if hp > 0:
                    P.wait("sw", "dve", prev_scale_done)
                for x in range(c0, c1 + 1):
                    xl = max(lo, HSH * x)
                    xr = min(lo + LNW, HSH * (x + 1))
                    P.op("sw", "dma_start", inc="sw",
                         out=v_grec[:, :, xl - lo:xr - lo],
                         in_=aout[x].rearrange("(i q) t -> q i t", q=130)
                         [128:130, :, xl - HSH * x:xr - HSH * x])
                grec_rdy = P.ckpt("sw")
            if hp in early_gsb:
                gsb_ck = {i: early_gsb[hp][1] for i in range(8)}
            else:
                gsb_ck = {}
                for i in range(8):
                    for x in range(c0, c1 + 1):
                        xl = max(lo, HSH * x)
                        xr = min(lo + LNW, HSH * (x + 1))
                        P.op("sw", "dma_start", inc="sw",
                             out=v_gsb[:, i, xl:xr],
                             in_=aout[x][130 * i:130 * i + 128, xl - HSH * x:
                                         xr - HSH * x])
                    gsb_ck[i] = P.ckpt("sw")
            for i in range(8):
                bank = psA if i % 2 == 0 else psB
                P.wait("pe", "swc", grec_rdy)
                if i >= 2:
                    P.wait("pe", "dve", dve_scale[i - 2])
                elif hp > 0:
                    P.wait("pe", "dve", bank_free)
                for u in range(NU):
                    P.op("pe", "matmul", bank[:, 512 * u:512 * u + 512], v_sel2,
                         v_grec[:, i, 512 * u:512 * u + 512],
                         start=True, stop=True,
                         inc="pe" if u == NU - 1 else None)
                pe_bc = P.n["pe"]
                P.wait("dve", "pe", pe_bc)
                P.wait("dve", "swc", gsb_ck[i])
                P.op("dve", "tensor_tensor", v_gsb[:, i, lo:lo + LNW],
                     v_gsb[:, i, lo:lo + LNW], bank[:, 0:LNW], ALU.mult)
                dve_scale[i] = P.n["dve"]
            prev_scale_done = dve_scale[7]
            pe_scale_all = pe_bc

            if hp > 0:
                P.wait("hw", "dve", prev_evac_done)
            for jj in range(4):
                P.op("hw", "dma_start", inc="hw", out=v_xres[:, jj, 0:LNW],
                     in_=xres_r[:, jj, lo:lo + LNW])
            xres_all = P.ckpt("hw")
            dve_evac = {}
            xres_ck_of = {jj: xres_all for jj in range(4)}
            for jj in range(8):
                if jj >= 4:
                    P.wait("hw", "dve", dve_evac[jj - 4])
                    P.op("hw", "dma_start", inc="hw",
                         out=v_xres[:, jj % 4, 0:LNW],
                         in_=xres_r[:, jj, lo:lo + LNW])
                    xres_ck_of[jj] = P.ckpt("hw")
                bank = psA if jj % 2 == 0 else psB
                if jj == 0:
                    P.wait("pe", "swc", wo_ck)
                P.wait("pe", "dve", dve_scale[7])
                if jj >= 2:
                    P.wait("pe", "dve", dve_evac[jj - 2])
                for i in range(8):
                    for u in range(NU):
                        P.op("pe", "matmul", bank[:, 512 * u:512 * u + 512],
                             v_wo[:, i, 128 * jj:128 * jj + 128],
                             v_gsb[:, i, lo + 512 * u:lo + 512 * u + 512],
                             start=(i == 0), stop=(i == 7),
                             skip_group_check=True,
                             inc="pe" if (i == 7 and u == NU - 1) else None)
                pe_oj = P.n["pe"]
                P.wait("dve", "pe", pe_oj)
                P.wait("dve", "hwc", xres_ck_of[jj])
                P.op("dve", "scalar_tensor_tensor", v_rr[:, jj, lo:lo + LNW],
                     bank[:, 0:LNW], v_bo[:, jj:jj + 1],
                     v_xres[:, jj % 4, 0:LNW], ALU.add, ALU.add)
                dve_evac[jj] = P.n["dve"]
                if deferred_norm:
                    deferred_norm.pop(0)()
            while deferred_norm:
                deferred_norm.pop(0)()
            prev_evac_done = dve_evac[7]
            pe_outproj = P.n["pe"]
            r_done = P.n["dve"]

            bc, slot = emit_ln_stats(lambda j: v_rr[:, j, :], lo, LNW,
                                     pre_dve=r_done,
                                     prev_stats_pe=pe_outproj)
            if hp == 0:
                ln1_rows = {}
                for q_ in range(8):
                    norm_chunk(lambda z: v_rr[:, z, :], lo, LNW, slot, q_,
                               dst_bf_of_j=lambda z: v_gsb[:, z, :])
                    ln1_rows[q_] = P.n["act"]
                ln1_bc0 = bc
                ln1_state[hp] = ("done",)
                if NPASS == 2:
                    load_pair(0, wait_pe=pe_scale_all)
                    emit_gate_up(0, lo, j0_dve=bc, act_rows=ln1_rows)
                    emit_gate_up(1, lo)
                    hoisted = True
                    bank_free = dve_ffnh[1]
                else:
                    bank_free = bc
            else:
                ln1_state[hp] = ("stats", slot, lo)
                bank_free = bc
            prev_win_dve = bc
        pe_outproj_all = P.n["pe"]

        if ln1_state.get(0, (None,))[0] == "done" and not hoisted:
            prev_sp_dve = max(prev_win_dve, ln1_bc0)
        else:
            prev_sp_dve = prev_win_dve
        ffn_hoisted = False
        for hp in range(NPASS):
            if True:
                fo = LNW * hp
                if hp + 1 in ln1_state and ln1_state[hp + 1][0] == "stats":
                    _, nslot, nlo = ln1_state[hp + 1]
                    for q_ in range(8):
                        deferred_norm.append(
                            (lambda q=q_, nslot=nslot, nlo=nlo: norm_chunk(
                                lambda z: v_rr[:, z, :], nlo, LNW, nslot, q,
                                dst_bf_of_j=lambda z: v_gsb[:, z, :])))
                    ln1_state[hp + 1] = ("pending",)

                if hp == 0 and hoisted:
                    j_start = 2
                    load_pair(1, wait_pe=pe_scale_all)
                    load_pair(2, wait_dve=dve_ffnh[1])
                elif hp > 0:
                    j_start = 0
                    load_pair(1, wait_pe=prev_gu_pe)
                else:
                    j_start = 0
                    load_pair(0, wait_pe=pe_scale_all)
                    load_pair(1)
                for j in range(j_start, 24):
                    if deferred_norm and j >= 4:
                        deferred_norm.pop(0)()
                    emit_gate_up(j, fo, j0_dve=prev_sp_dve,
                                 act_rows=(ln1_rows if hp == 0 and
                                           not hoisted and j == 0 else None),
                                 first=(j == j_start))
                    if j % 2 == 1 and j // 2 + 2 < 12:
                        load_pair(j // 2 + 2, wait_dve=dve_ffnh[j])
                while deferred_norm:
                    deferred_norm.pop(0)()
                act_norm_cover = P.n["act"]
                ffnh_done = P.n["dve"]
                prev_gu_pe = P.n["pe"]
                ovl = True
                ln2s_mm = {}
                ln2_rbf = {}

                def emit_ln2_stats_j(q):
                    P.wait("pe", "dve", ln2_rbf[q][0])
                    P.wait("pe", "act", ln2_rbf[q][1])
                    for u in range(fnu):
                        P.op("pe", "matmul",
                             psB[0:1, 1024 + 512 * u:1024 + 512 * u + 512],
                             v_ones, v_rbf[:, q % 2, 512 * u:512 * u + 512],
                             start=(q == 0), stop=(q == 7),
                             skip_group_check=True)
                        P.op("pe", "matmul",
                             psA[0:1, 1024 + 512 * u:1024 + 512 * u + 512],
                             v_ones, v_sqb[:, q % 2, 512 * u:512 * u + 512],
                             start=(q == 0), stop=(q == 7),
                             skip_group_check=True,
                             inc="pe" if u == fnu - 1 else None)
                    ln2s_mm[q] = P.n["pe"]
                dve_r2 = {}
                pe_dn_of = {}
                for j in range(8):
                    sl = j % 2
                    if j >= 2:
                        P.wait("sw", "dve", dve_r2[j - 2])
                    elif j == 0:
                        P.wait("sw", "pe", pe_gu_of[21])
                    else:
                        P.wait("sw", "pe", prev_gu_pe)
                    P.op("sw", "dma_start", inc="sw", out=v_wd[sl],
                         in_=wdT_r[:, :, 128 * j:128 * j + 128])
                    wd_rdy = P.ckpt("sw")
                    if j == 7 and hp + 1 < NPASS:
                        load_pair(0, wait_pe=pe_dn_of[6])
                    bank = psA if sl == 0 else psB
                    P.wait("pe", "swc", wd_rdy)
                    P.wait("pe", "dve", ffnh_done)
                    if j >= 2:
                        P.wait("pe", "dve", dve_r2[j - 2])
                    for hch in range(24):
                        for u in range(fnu):
                            P.op("pe", "matmul", bank[:, 512 * u:512 * u + 512],
                                 v_wd[sl][:, hch, :],
                                 ffnh[hch][:, 512 * u:512 * u + 512],
                                 start=(hch == 0), stop=(hch == 23),
                                 skip_group_check=True,
                                 inc="pe" if (hch == 23 and u == fnu - 1)
                                 else None)
                    if ovl:
                        emit_ln2_stats_j(j - 2) if j >= 2 else None
                    pe_dn = P.n["pe"]
                    pe_dn_of[j] = pe_dn
                    P.wait("dve", "pe", pe_dn)
                    P.edge("dve")
                    P.op("dve", "scalar_tensor_tensor", v_tmp[0:128, 0:FW],
                         bank[:, 0:FW], DEQ, v_rr[:, j, fo:fo + FW],
                         ALU.mult, ALU.add)
                    P.edge("dve")
                    P.op("dve", "tensor_copy", v_rr[:, j, fo:fo + FW],
                         v_tmp[0:128, 0:FW])
                    dve_r2[j] = P.n["dve"]
                    if ovl:
                        if j >= 2:
                            P.wait("dve", "pe", ln2s_mm[j - 2])
                        P.edge("dve")
                        P.op("dve", "tensor_copy", v_rbf[:, j % 2, 0:FW],
                             v_rr[:, j, fo:fo + FW])
                        P.wait("act", "dve", dve_r2[j])
                        if j >= 2:
                            P.wait("act", "pe", ln2s_mm[j - 2])
                        P.op("act", "activation", v_sqb[:, j % 2, 0:FW],
                             v_rr[:, j, fo:fo + FW], AF.Square)
                        ln2_rbf[j] = (P.n["dve"], P.n["act"])
                r2_done = P.n["dve"]
                pe_dn_all = P.n["pe"]
                ffn_hoisted = False
                ln2_slot = ln_slot_ctr[0] % 2
                if ovl:
                    ln_slot_ctr[0] += 1
                    emit_ln2_stats_j(6)
                    emit_ln2_stats_j(7)
                    pe_stats = ln2s_mm[7]
                    P.wait("dve", "pe", pe_stats)
                    P.op("dve", "tensor_scalar_mul", v_rowa[:, 0:FW],
                         psB[0:1, 1024:1024 + FW], 1.0 / D)
                    P.edge("dve")
                    P.op("dve", "tensor_tensor", v_rowb[:, 0:FW],
                         v_rowa[:, 0:FW], v_rowa[:, 0:FW], ALU.mult)
                    P.edge("dve")
                    P.op("dve", "scalar_tensor_tensor", v_rowc[:, 0:FW],
                         psA[0:1, 1024:1024 + FW], 1.0 / D, v_rowb[:, 0:FW],
                         ALU.mult, ALU.subtract)
                    dvar = P.n["dve"]
                    P.wait("act", "dve", dvar)
                    P.op("act", "activation", v_rowb[:, 0:FW],
                         v_rowc[:, 0:FW], AF.Ln, bias=v_eps[:])
                    P.edge("act")
                    mrow = v_rowbf[:, 0:FW]
                    rrow = v_rowbf[:, LNW:LNW + FW]
                    P.op("act", "activation", rrow, v_rowb[:, 0:FW],
                         AF.Exp, scale=-0.5)
                    P.op("act", "activation", mrow, v_rowa[:, 0:FW], AF.Copy)
                    a_rows = P.n["act"]
                    P.wait("pe", "act", a_rows)
                    for u in range(FW // 512):
                        P.op("pe", "matmul", psA[:, 512 * u:512 * u + 512],
                             v_onesr, mrow[:, 512 * u:512 * u + 512],
                             start=True, stop=True)
                        P.op("pe", "matmul", psB[:, 512 * u:512 * u + 512],
                             v_onesr, rrow[:, 512 * u:512 * u + 512],
                             start=True, stop=True,
                             inc="pe" if u == FW // 512 - 1 else None)
                    pe_bcast = P.n["pe"]
                    P.wait("dve", "pe", pe_bcast)
                    P.op("dve", "tensor_copy",
                         v_mbc[:, LNW * ln2_slot:LNW * ln2_slot + FW],
                         psA[:, 0:FW])
                    P.op("dve", "tensor_copy",
                         v_rbc[:, LNW * ln2_slot:LNW * ln2_slot + FW],
                         psB[:, 0:FW])
                    ln2_bc = P.n["dve"]
                else:
                    ln2_bc, ln2_slot = emit_ln_stats(
                        lambda j: v_rr[:, j, :], fo, FW,
                        pre_dve=r2_done, prev_stats_pe=pe_dn_all)
                if hp + 1 < NPASS:
                    for q_ in range(8):
                        deferred_norm.append(
                            (lambda q=q_, sl_=ln2_slot, fo_=fo: norm_chunk(
                                lambda z: v_rr[:, z, :], fo_, FW, sl_, q,
                                out_dma_of_j=lambda z:
                                outr[:, z, fo_:fo_ + FW])))
                else:
                    emit_ln_norm(lambda j: v_rr[:, j, :], fo, FW, ln2_slot,
                                 out_dma_of_j=lambda j: outr[:, j, fo:fo + FW])
                prev_sp_dve = ln2_bc
                prev_gu_pe = pe_dn_all

        out_ckpt = P.ckpt("hw")
        P.wait("sw", "hwc", out_ckpt)
        P.wait("sw", "sw", P.n["sw"])

        if not dense_sync:
            P.finalize_device()
        nops = {k: len(P.q[k]) for k in ENGS}
        nincs = {k: sum(1 for it in P.q[k] if it[0] == "o" and it[4])
                 for k in ENGS}
        print(f"[kernel] plan ops: {nops}; incs: {nincs}; counters {P.n}")

        def runner(name):
            def run(h):
                for item in P.q[name]:
                    if item[0] == "w":
                        _, sname, val = item
                        h.wait_ge(sem[sname], val)
                    elif item[0] == "o":
                        _, method, args, kw, inc = item
                        ins = getattr(h, method)(*args, **kw)
                        if inc is not None:
                            ins.then_inc(sem[inc],
                                         16 if inc in ("hw", "sw") else 1)
                    else:
                        _, fn, inc = item
                        ins = fn(h)
                        ins.then_inc(sem[inc])
            return run

        block = e(nc.Block())
        block.tensor(runner("pe"))
        block.scalar(runner("act"))
        block.vector(runner("dve"))
        block.sync(runner("hw"))
        block.gpsimd(runner("sw"))

    return nc


def stage_inputs(inputs, T):
    x = np.asarray(inputs["L0_attn_x"], np.float32)
    wq = np.asarray(inputs["L0_attn_wq"], np.float32)
    bq = np.asarray(inputs["L0_attn_bq"], np.float32)
    wk = np.asarray(inputs["L0_attn_wk"], np.float32)
    bk = np.asarray(inputs["L0_attn_bk"], np.float32)
    wv = np.asarray(inputs["L0_attn_wv"], np.float32)
    bv = np.asarray(inputs["L0_attn_bv"], np.float32)
    wo = np.ascontiguousarray(np.asarray(inputs["L0_attn_wo"], np.float32))
    bo = np.asarray(inputs["L0_attn_bo"], np.float32)
    wg = np.asarray(inputs["L0_ffn_w_gate"], np.int32)
    wu = np.asarray(inputs["L0_ffn_w_up"], np.int32)
    wd = np.asarray(inputs["L0_ffn_w_down"], np.int32)

    TOKC = 2 * T // NCORE
    xT = [np.ascontiguousarray(x[b].T) for b in range(2)]
    xS = [np.ascontiguousarray(
        xT[b].reshape(8, 128, T // 1024, 2, 512)
        .transpose(1, 2, 3, 0, 4).reshape(128, -1)) for b in range(2)]
    wgT = np.ascontiguousarray(wg.T)
    wuT = np.ascontiguousarray(wu.T)
    wdT = np.ascontiguousarray(wd.T)
    bo_c = np.ascontiguousarray(bo.reshape(8, 128).T)

    def qkv_s(w, hsl):
        return np.ascontiguousarray(
            w[:, hsl].reshape(8, 128, 128).transpose(1, 0, 2).reshape(128, -1))

    masks = np.zeros((512, 1024), ml_dtypes.bfloat16)
    for j in range(4):
        k = np.arange(128)[:, None]
        q = np.arange(512)[None, :]
        m = (q >= 128 * j + k).astype(ml_dtypes.bfloat16)
        masks[128 * j:128 * j + 128, 0:512] = m
        masks[128 * j:128 * j + 128, 512:1024] = m
    ones = np.ones((128, 1), np.float32)
    onesr = np.ones((1, 128), np.float32)
    sel2 = np.zeros((2, 128), np.float32)
    sel2[0, 0:64] = 1.0
    sel2[1, 64:128] = 1.0

    in_maps = []
    for c in range(NCORE):
        b = (c * TOKC) // T
        t0 = (c * TOKC) % T
        hsl = slice(128 * c, 128 * c + 128)
        in_maps.append({
            "xT0": xS[0], "xT1": xS[1],
            "xres": np.ascontiguousarray(xT[b][:, t0:t0 + TOKC]),
            "wq": qkv_s(wq, hsl),
            "wk": qkv_s(wk, hsl),
            "wv": qkv_s(wv, hsl),
            "bq": np.ascontiguousarray(bq[hsl].reshape(128, 1)),
            "bk": np.ascontiguousarray(bk[hsl].reshape(128, 1)),
            "bvrow": np.ascontiguousarray(bv[hsl].reshape(1, 128)),
            "wo": wo, "bo": bo_c,
            "wgT": wgT, "wuT": wuT, "wdT": wdT,
            "masks": masks, "onescol": ones, "onesrow": onesr, "sel2": sel2,
        })
    return in_maps


def assemble_output(results, T):
    TOKC = 2 * T // NCORE
    out = np.empty((2, T, D), np.float32)
    for c in range(NCORE):
        b = (c * TOKC) // T
        t0 = (c * TOKC) % T
        out[b, t0:t0 + TOKC, :] = np.asarray(results[c]["outT"]).T
    return out


def kernel(**inputs):
    T = int(np.asarray(inputs["L0_attn_x"]).shape[1])
    in_maps = stage_inputs(inputs, T)
    nc = build_nc(T)
    res = run_bass_kernel_spmd(nc, in_maps, core_ids=list(range(NCORE)))
    return assemble_output(res.results, T)
